# version 33
# baseline (speedup 1.0000x reference)
import math
from contextlib import ExitStack

import numpy as np
import ml_dtypes
import concourse.bass as bass
import concourse.mybir as mybir
from concourse.bass_utils import run_bass_kernel_spmd

F32 = mybir.dt.float32
BF16 = mybir.dt.bfloat16
FP8 = mybir.dt.float8e4
I32 = mybir.dt.int32
AF = mybir.ActivationFunctionType
ALU = mybir.AluOpType
AXL = mybir.AxisListType

B, S, D, H, HD = 4, 4096, 1024, 16, 64
TQ = S // 2
NT = S // 128
NTQ = TQ // 128
EPS = 1e-5
QB = 512
NQB = TQ // QB
NKT = S // 128
VGH = 4
VSL = HD + 1
VROW = VGH * VSL

ENGINES = ("tensor", "vector", "scalar", "gpsimd", "sync")


class _Res:
    __slots__ = ("w", "rs", "name")

    def __init__(self, name=""):
        self.w = None
        self.rs = []
        self.name = name


class _Op:
    __slots__ = ("eng", "fn", "deps", "is_dma", "need_inc", "sem_val", "idx",
                 "is_bar", "dma_slot", "dma_n")

    def __init__(self, eng, fn, is_dma, idx):
        self.eng = eng
        self.fn = fn
        self.deps = []
        self.is_dma = is_dma
        self.need_inc = False
        self.sem_val = None
        self.idx = idx
        self.is_bar = False
        self.dma_slot = 0
        self.dma_n = -1


NDMASEM = 8


class Sched:

    def __init__(self):
        self.ops = {e: [] for e in ENGINES}
        self.n = 0

    def op(self, eng, fn, reads=(), writes=(), is_dma=False):
        o = _Op(eng, fn, is_dma, self.n)
        self.n += 1
        for r in reads:
            if r.w is not None:
                o.deps.append(r.w)
            r.rs.append(o)
        for r in writes:
            if r.w is not None:
                o.deps.append(r.w)
            o.deps.extend(r.rs)
            r.rs = []
            r.w = o
        self.ops[eng].append(o)
        return o

    def barrier(self):
        rb = {e: _Res(f"bar{self.n}_{e}") for e in ENGINES}
        for e in ENGINES:
            o = self.op(e, lambda eng: eng.nop(nofuse=True), writes=[rb[e]])
            o.is_bar = True
        allr = list(rb.values())
        for e in ENGINES:
            o = self.op(e, lambda eng: eng.nop(nofuse=True), reads=allr)
            o.is_bar = True

    def emit(self, nc, block):
        for e in ENGINES:
            for o in self.ops[e]:
                for d in o.deps:
                    if d.eng != o.eng or d.is_dma != o.is_dma:
                        d.need_inc = True
        for e in ENGINES:
            cnt_c = 0
            n_dma = 0
            cnt_b = 0
            for o in self.ops[e]:
                if o.is_dma:
                    o.need_inc = True
                    o.dma_slot = n_dma % NDMASEM
                    o.sem_val = 16 * (n_dma // NDMASEM + 1)
                    o.dma_n = n_dma
                    n_dma += 1
                elif o.is_bar and o.need_inc:
                    cnt_b += 1
                    o.sem_val = cnt_b
                elif o.need_inc:
                    cnt_c += 1
                    o.sem_val = cnt_c
        csem = {e: nc.semaphore(f"c_{e}").__enter__() for e in ENGINES}
        bsem = {e: nc.semaphore(f"b_{e}").__enter__() for e in ENGINES}
        dsem = {}
        for e in ENGINES:
            if any(o.is_dma for o in self.ops[e]):
                dsem[e] = [nc.semaphore(f"d_{e}{i}").__enter__()
                           for i in range(NDMASEM)]
        sched = self

        local_of = {}
        for e in ENGINES:
            for li, o in enumerate(self.ops[e]):
                local_of[id(o)] = li

        def runner(ename):
            def body(eng):
                waited = {}
                drained = -1
                prev_local = -1
                n_dma_seen = 0
                for o in sched.ops[ename]:
                    if o.is_bar and ename in dsem:
                        for j in range(NDMASEM):
                            cnt = (n_dma_seen - j + NDMASEM - 1) // NDMASEM
                            if cnt <= 0:
                                continue
                            v = 16 * cnt
                            key = (ename, "d", j)
                            if waited.get(key, -1) >= v:
                                continue
                            waited[key] = v
                            eng.wait_ge(dsem[ename][j], v)
                    need_drain = False
                    for d in o.deps:
                        if (d.eng == o.eng and d.is_dma and o.is_dma):
                            continue
                        if d.eng == o.eng and not d.is_dma and not o.is_dma:
                            if (ename in ("vector", "scalar", "gpsimd")
                                    and local_of[id(d)] > drained):
                                need_drain = True
                            continue
                        if d.is_dma:
                            for j in range(NDMASEM):
                                cnt = (d.dma_n + 1 - j + NDMASEM - 1) // NDMASEM
                                if cnt <= 0:
                                    continue
                                v = 16 * cnt
                                key = (d.eng, "d", j)
                                if waited.get(key, -1) >= v:
                                    continue
                                waited[key] = v
                                eng.wait_ge(dsem[d.eng][j], v)
                            continue
                        if d.is_bar:
                            sem = bsem[d.eng]
                            key = (d.eng, "b")
                        else:
                            sem = csem[d.eng]
                            key = (d.eng, "c")
                        v = d.sem_val
                        assert v is not None
                        if waited.get(key, -1) >= v:
                            continue
                        waited[key] = v
                        eng.wait_ge(sem, v)
                    if need_drain:
                        eng.drain()
                        drained = prev_local
                    if o.is_dma and o.sem_val > 16:
                        key = (ename, "d", o.dma_slot)
                        v = o.sem_val - 16
                        if waited.get(key, -1) < v:
                            waited[key] = v
                            eng.wait_ge(dsem[ename][o.dma_slot], v)
                    inst = o.fn(eng)
                    if o.is_dma:
                        n_dma_seen += 1
                    else:
                        prev_local = local_of[id(o)]
                    if o.need_inc:
                        assert inst is not None, (ename, o.idx)
                        if o.is_dma:
                            inst.then_inc(dsem[ename][o.dma_slot], 16)
                        elif o.is_bar:
                            inst.then_inc(bsem[ename], 1)
                        else:
                            inst.then_inc(csem[ename], 1)
            return body

        block.tensor(runner("tensor"))
        block.vector(runner("vector"))
        block.scalar(runner("scalar"))
        block.gpsimd(runner("gpsimd"))
        block.sync(runner("sync"))


def build_program(taps=(), phases=3):
    nc = bass.Bass("TRN2", target_bir_lowering=False)
    taps = set(taps)
    R = _Res
    sch = Sched()

    dx = nc.dram_tensor("x", [S, D], F32, kind="ExternalInput").ap()
    dwq = nc.dram_tensor("wqt", [D, D], F32, kind="ExternalInput").ap()
    dwk = nc.dram_tensor("wkt", [D, D], F32, kind="ExternalInput").ap()
    dwv = nc.dram_tensor("wvt", [D, D], F32, kind="ExternalInput").ap()
    dwo = nc.dram_tensor("wot", [D, D], F32, kind="ExternalInput").ap()
    dkern = nc.dram_tensor("kern", [D, D], BF16, kind="ExternalInput").ap()
    dbq = nc.dram_tensor("bqp", [128, 8], F32, kind="ExternalInput").ap()
    dbk = nc.dram_tensor("bkp", [128, 8], F32, kind="ExternalInput").ap()
    dbv = nc.dram_tensor("bvr", [1, D], F32, kind="ExternalInput").ap()
    dbo = nc.dram_tensor("bor", [1, D], F32, kind="ExternalInput").ap()
    dfb = nc.dram_tensor("fbias", [128, D], BF16, kind="ExternalInput").ap()
    dcos = nc.dram_tensor("cos_t", [S, D // 2], F32, kind="ExternalInput").ap()
    dsin = nc.dram_tensor("sin_t", [S, D // 2], F32, kind="ExternalInput").ap()
    did8 = nc.dram_tensor("id8", [128, 128], FP8, kind="ExternalInput").ap()
    didb = nc.dram_tensor("idb", [128, 128], BF16, kind="ExternalInput").ap()
    dsin_s = nc.dram_tensor("s_in", [128, 1], F32, kind="ExternalInput").ap()
    dsk_s = nc.dram_tensor("s_k", [128, 1], F32, kind="ExternalInput").ap()

    dout = nc.dram_tensor("out", [TQ, D], F32, kind="ExternalOutput").ap()

    dx1 = nc.dram_tensor("x1_sc", [TQ, D], F32).ap()
    drxh = nc.dram_tensor("rxh_sc", [8, 128, S], BF16).ap()
    drxl = nc.dram_tensor("rxl_sc", [8, 128, S], BF16).ap()
    dwsp = nc.dram_tensor("wsp_sc", [3, 2, 8, 128, D], BF16).ap()
    dwosp = nc.dram_tensor("wosp_sc", [2, 8, 128, D], BF16).ap()
    da = nc.dram_tensor("a_sc", [2, D, TQ], BF16).ap()

    tap_aps = {}

    def add_tap(name, shape, dt):
        if name in taps:
            tap_aps[name] = nc.dram_tensor("tap_" + name, shape, dt,
                                           kind="ExternalOutput").ap()

    add_tap("x1", [TQ, D], F32)
    add_tap("rxh", [8, 128, S], BF16)
    add_tap("rxl", [8, 128, S], BF16)
    add_tap("a", [2, D, TQ], BF16)

    with ExitStack() as top:
        ent = top.enter_context

        kern8 = ent(nc.sbuf_tensor("kern8", [128, 8 * D], FP8))
        fb_sb = ent(nc.sbuf_tensor("fb_sb", [128, D], BF16))
        bq_sb = ent(nc.sbuf_tensor("bq_sb", [128, 8], F32))
        bk_sb = ent(nc.sbuf_tensor("bk_sb", [128, 8], F32))
        bvrows = ent(nc.sbuf_tensor("bvrows", [2, D], BF16))
        borows = ent(nc.sbuf_tensor("borows", [2, D], BF16))
        ones2 = ent(nc.sbuf_tensor("ones2", [2, 128], BF16))
        id8_sb = ent(nc.sbuf_tensor("id8_sb", [128, 128], FP8))
        idb_sb = ent(nc.sbuf_tensor("idb_sb", [128, 128], BF16))
        sinv_sb = ent(nc.sbuf_tensor("sinv_sb", [128, 1], F32))
        stot_sb = ent(nc.sbuf_tensor("stot_sb", [128, 1], F32))
        scs = ent(nc.sbuf_tensor("scs", [128, 16], F32))
        ln_dump = ent(nc.sbuf_tensor("ln_dump", [128, D], F32))

        r_kern8, r_fb, r_bq, r_bk = R("kern8"), R("fb"), R("bq"), R("bk")
        r_bvrows, r_borows, r_ones2 = R("bvr"), R("bor"), R("ones2")
        r_id8, r_idb, r_sinv, r_stot = R("id8"), R("idb"), R("sinv"), R("stot")
        r_scs, r_dump = R("scs"), R("dump")
        r_dx1, r_drx, r_dwsp, r_dwosp, r_da, r_out = (
            R("dx1"), R("drx"), R("dwsp"), R("dwosp"), R("da"), R("out"))

        with ExitStack() as pre:
            pe = pre.enter_context
            kstage = pe(nc.sbuf_tensor("kstage", [128, 8 * D], BF16))
            brs = pe(nc.sbuf_tensor("brs", [1, D], F32))
            brl = pe(nc.sbuf_tensor("brl", [1, D], BF16))
            r_kstage, r_brs, r_brl = R(), R(), R()

            for dst, src, rr in ((id8_sb, did8, r_id8), (idb_sb, didb, r_idb),
                                 (fb_sb, dfb, r_fb), (bq_sb, dbq, r_bq),
                                 (bk_sb, dbk, r_bk)):
                sch.op("sync", (lambda dst, src: lambda e: e.dma_start(
                    out=dst[:, :], in_=src))(dst, src), writes=[rr], is_dma=True)
            sch.op("sync", lambda e: e.dma_start(out=scs[:, 0:1], in_=dsin_s),
                   writes=[r_scs], is_dma=True)
            sch.op("sync", lambda e: e.dma_start(out=scs[:, 1:2], in_=dsk_s),
                   writes=[r_scs], is_dma=True)
            sch.op("sync", lambda e: e.dma_start(
                out=kstage[:, :], in_=dkern.rearrange("(c p) d -> p c d", p=128)),
                writes=[r_kstage], is_dma=True)
            sch.op("vector", lambda e: e.memset(ones2[:, :], 1.0), writes=[r_ones2])

            def recip_ops(src_col, dst_col):
                sch.op("scalar", lambda e: e.activation(scs[:, 4:5], src_col, AF.Ln),
                       reads=[r_scs], writes=[r_scs])
                sch.op("scalar", lambda e: e.activation(scs[:, 5:6], scs[:, 4:5],
                                                        AF.Exp, scale=-1.0),
                       reads=[r_scs], writes=[r_scs])
                for _ in range(2):
                    sch.op("vector", lambda e: e.tensor_mul(scs[:, 4:5], src_col,
                                                            scs[:, 5:6]),
                           reads=[r_scs], writes=[r_scs])
                    sch.op("vector", lambda e: e.tensor_scalar(
                        scs[:, 4:5], scs[:, 4:5], -1.0, 2.0, ALU.mult, ALU.add),
                        reads=[r_scs], writes=[r_scs])
                    sch.op("vector", lambda e: e.tensor_mul(scs[:, 5:6], scs[:, 5:6],
                                                            scs[:, 4:5]),
                           reads=[r_scs], writes=[r_scs])
                sch.op("vector", lambda e: e.tensor_copy(dst_col, scs[:, 5:6]),
                       reads=[r_scs], writes=[r_scs])

            recip_ops(scs[:, 0:1], scs[:, 6:7])
            recip_ops(scs[:, 1:2], scs[:, 7:8])
            sch.op("vector", lambda e: e.tensor_copy(sinv_sb[:, :], scs[:, 6:7]),
                   reads=[r_scs], writes=[r_sinv])
            sch.op("vector", lambda e: e.tensor_mul(stot_sb[:, :], scs[:, 0:1],
                                                    scs[:, 1:2]),
                   reads=[r_scs], writes=[r_stot])
            sch.op("vector", lambda e: e.tensor_scalar(
                kern8[:, :], kstage[:, :], scs[:, 7:8], None, ALU.mult),
                reads=[r_kstage, r_scs], writes=[r_kern8])
            sch.op("vector", lambda e: e.tensor_copy(
                kern8[:, :].bitcast(I32), kern8[:, :].bitcast(I32)),
                reads=[r_kern8], writes=[r_kern8])

            for src, dst, rr in ((dbv, bvrows, r_bvrows), (dbo, borows, r_borows)):
                def _bias(src=src, dst=dst, rr=rr):
                    sch.op("sync", lambda e: e.dma_start(out=brs[0:1, :], in_=src),
                           writes=[r_brs], is_dma=True)
                    sch.op("vector", lambda e: e.tensor_copy(dst[0:1, :], brs[0:1, :]),
                           reads=[r_brs], writes=[rr])
                    sch.op("vector", lambda e: e.scalar_tensor_tensor(
                        brl[0:1, :], brs[0:1, :], 1.0, dst[0:1, :],
                        ALU.mult, ALU.subtract),
                        reads=[r_brs, rr], writes=[r_brl])
                    sch.op("gpsimd", lambda e: e.dma_start(out=dst[1:2, :],
                                                           in_=brl[0:1, :]),
                           reads=[r_brl], writes=[rr], is_dma=True)
                _bias()
            sch.barrier()

        with ExitStack() as pha:
            pa = pha.enter_context
            NB = 4
            xt = [pa(nc.sbuf_tensor(f"xt{i}", [128, D], F32)) for i in range(NB)]
            xb = [pa(nc.sbuf_tensor(f"xb{i}", [128, D], BF16)) for i in range(NB)]
            xqT = [pa(nc.sbuf_tensor(f"xqT{i}", [128, D], FP8)) for i in range(NB)]
            ybf = [pa(nc.sbuf_tensor(f"ybf{i}", [128, D], BF16)) for i in range(NB)]
            x1t = [pa(nc.sbuf_tensor(f"x1t{i}", [128, D], F32)) for i in range(NB)]
            rxf = [pa(nc.sbuf_tensor(f"rxf{i}", [128, D], F32)) for i in range(NB)]
            rxh_t = [pa(nc.sbuf_tensor(f"rxht{i}", [128, D], BF16)) for i in range(NB)]
            rxl_t = [pa(nc.sbuf_tensor(f"rxlt{i}", [128, D], BF16)) for i in range(NB)]
            rxhT = [pa(nc.sbuf_tensor(f"rxhT{i}", [128, D], BF16)) for i in range(NB)]
            rxlT = [pa(nc.sbuf_tensor(f"rxlT{i}", [128, D], BF16)) for i in range(NB)]
            cost = [pa(nc.sbuf_tensor(f"cost{i}", [128, D // 2], F32)) for i in range(NB)]
            sint = [pa(nc.sbuf_tensor(f"sint{i}", [128, D // 2], F32)) for i in range(NB)]
            rp1 = [pa(nc.sbuf_tensor(f"rp1_{i}", [128, D // 2], F32)) for i in range(NB)]
            rp2 = [pa(nc.sbuf_tensor(f"rp2_{i}", [128, D // 2], F32)) for i in range(NB)]
            stt = [pa(nc.sbuf_tensor(f"stt{i}", [128, 12], F32)) for i in range(NB)]
            wst = [pa(nc.sbuf_tensor(f"wst{i}", [128, D], F32)) for i in range(NB)]
            wsh = [pa(nc.sbuf_tensor(f"wsh{i}", [128, D], BF16)) for i in range(NB)]
            wsl = [pa(nc.sbuf_tensor(f"wsl{i}", [128, D], BF16)) for i in range(NB)]
            psY = [pa(nc.psum_tensor(f"psY{i}", [128, D], F32))
                   for i in range(2)]
            psT8 = [pa(nc.psum_tensor("psT8_0", [128, D], BF16))]
            psTh = [pa(nc.psum_tensor("psTh0", [128, D], BF16))]
            psTl = [pa(nc.psum_tensor("psTl0", [128, D], BF16))]

            r_xt = [R() for _ in range(NB)]
            r_xb = [R() for _ in range(NB)]
            r_xqT = [R() for _ in range(NB)]
            r_ybf = [R() for _ in range(NB)]
            r_x1t = [R() for _ in range(NB)]
            r_rxf = [R() for _ in range(NB)]
            r_rxh = [R() for _ in range(NB)]
            r_rxl = [R() for _ in range(NB)]
            r_rxhT = [R() for _ in range(NB)]
            r_rxlT = [R() for _ in range(NB)]
            r_cos = [R() for _ in range(NB)]
            r_sin = [R() for _ in range(NB)]
            r_rp1 = [R() for _ in range(NB)]
            r_rp2 = [R() for _ in range(NB)]
            r_stt = [R() for _ in range(NB)]
            r_wst = [R() for _ in range(NB)]
            r_wsh = [R() for _ in range(NB)]
            r_wsl = [R() for _ in range(NB)]
            r_psY = [R(), R()]
            r_psT8 = [R()]
            r_psTh = [R()]
            r_psTl = [R()]

            def layer_norm(src_ap, r_src_list, st, r_st, dst_ap, r_dst_list):
                rd = list(r_src_list)
                sch.op("vector", lambda e: e.reduce_sum(st[:, 0:1], src_ap, axis=AXL.X),
                       reads=rd, writes=[r_st])
                sch.op("vector", lambda e: e.scalar_tensor_tensor(
                    ln_dump[:, :], src_ap, 1.0, src_ap, ALU.mult, ALU.mult,
                    accum_out=st[:, 1:2]),
                    reads=rd, writes=[r_dump, r_st])
                sch.op("vector", lambda e: e.tensor_scalar(
                    st[:, 2:3], st[:, 0:1], -1.0 / D, None, ALU.mult),
                    reads=[r_st], writes=[r_st])
                sch.op("vector", lambda e: e.scalar_tensor_tensor(
                    st[:, 3:4], st[:, 2:3], 1.0, st[:, 2:3], ALU.mult, ALU.mult),
                    reads=[r_st], writes=[r_st])
                sch.op("vector", lambda e: e.scalar_tensor_tensor(
                    st[:, 4:5], st[:, 3:4], -D, st[:, 1:2], ALU.mult, ALU.add),
                    reads=[r_st], writes=[r_st])
                sch.op("vector", lambda e: e.tensor_scalar(
                    st[:, 4:5], st[:, 4:5], 1.0 / D, None, ALU.mult),
                    reads=[r_st], writes=[r_st])
                sch.op("vector", lambda e: e.tensor_scalar(
                    st[:, 9:10], st[:, 4:5], EPS, None, ALU.add),
                    reads=[r_st], writes=[r_st])
                sch.op("scalar", lambda e: e.activation(
                    st[:, 5:6], st[:, 9:10], AF.Ln),
                    reads=[r_st], writes=[r_st])
                sch.op("scalar", lambda e: e.activation(
                    st[:, 6:7], st[:, 5:6], AF.Exp, scale=-0.5),
                    reads=[r_st], writes=[r_st])
                sch.op("vector", lambda e: e.scalar_tensor_tensor(
                    st[:, 7:8], st[:, 6:7], 1.0, st[:, 6:7], ALU.mult, ALU.mult),
                    reads=[r_st], writes=[r_st])
                sch.op("vector", lambda e: e.scalar_tensor_tensor(
                    st[:, 8:9], st[:, 9:10], 1.0, st[:, 7:8], ALU.mult, ALU.mult),
                    reads=[r_st], writes=[r_st])
                sch.op("vector", lambda e: e.tensor_scalar(
                    st[:, 8:9], st[:, 8:9], -0.5, 1.5, ALU.mult, ALU.add),
                    reads=[r_st], writes=[r_st])
                sch.op("vector", lambda e: e.scalar_tensor_tensor(
                    st[:, 6:7], st[:, 8:9], 1.0, st[:, 6:7], ALU.mult, ALU.mult),
                    reads=[r_st], writes=[r_st])
                sch.op("vector", lambda e: e.tensor_scalar(
                    dst_ap, src_ap, st[:, 2:3], st[:, 6:7], ALU.add, ALU.mult),
                    reads=rd + [r_st], writes=list(r_dst_list))

            wsrc_list = [(dwq, 0), (dwk, 1), (dwv, 2), (dwo, 3)]
            HF = D // 2

            def stage0(t):
                i = t % NB
                sch.op("sync", lambda e, t=t, i=i: e.dma_start(
                    out=xt[i][:, :], in_=dx[t * 128:(t + 1) * 128, :]),
                    writes=[r_xt[i]], is_dma=True)
                sch.op("sync", lambda e, t=t, i=i: e.dma_start(
                    out=cost[i][:, :], in_=dcos[t * 128:(t + 1) * 128, :]),
                    writes=[r_cos[i]], is_dma=True)
                sch.op("sync", lambda e, t=t, i=i: e.dma_start(
                    out=sint[i][:, :], in_=dsin[t * 128:(t + 1) * 128, :]),
                    writes=[r_sin[i]], is_dma=True)
                wsrc = wsrc_list[t // 8][0]
                chunk = t % 8
                sch.op("sync", lambda e, wsrc=wsrc, chunk=chunk, i=i: e.dma_start(
                    out=wst[i][:, :], in_=wsrc[chunk * 128:(chunk + 1) * 128, :]),
                    writes=[r_wst[i]], is_dma=True)

            def stage1(t):
                i = t % NB
                tb = 0
                layer_norm(xt[i][:, :], [r_xt[i]], stt[i], r_stt[i],
                           xb[i][:, :], [r_xb[i]])
                for c in range(8):
                    sch.op("tensor", lambda e, i=i, c=c, tb=tb: e.transpose(
                        psT8[tb][:, c * 128:(c + 1) * 128],
                        xb[i][:, c * 128:(c + 1) * 128], idb_sb[:, :]),
                        reads=[r_xb[i], r_idb], writes=[r_psT8[tb]])
                sch.op("vector", lambda e, i=i, tb=tb: e.tensor_scalar(
                    xqT[i][:, :], psT8[tb][:, 0:D], sinv_sb[:, 0:1], None, ALU.mult),
                    reads=[r_psT8[tb], r_sinv], writes=[r_xqT[i]])
                sch.op("vector", lambda e, i=i: e.tensor_copy(
                    xqT[i][:, :].bitcast(I32), xqT[i][:, :].bitcast(I32)),
                    reads=[r_xqT[i]], writes=[r_xqT[i]])

            def stage2(t):
                i = t % NB
                yb = t % 2
                for half in range(2):
                    for c in range(8):
                        sch.op("tensor", lambda e, i=i, c=c, half=half, yb=yb: e.matmul(
                            psY[yb][:, half * 512:(half + 1) * 512],
                            xqT[i][:, c * 128:(c + 1) * 128],
                            kern8[:, c * D + half * 512:c * D + half * 512 + 512],
                            start=(c == 0), stop=(c == 7)),
                            reads=[r_xqT[i], r_kern8], writes=[r_psY[yb]])
                for half in range(2):
                    sch.op("vector", lambda e, i=i, half=half, yb=yb:
                           e.scalar_tensor_tensor(
                               ybf[i][:, half * 512:(half + 1) * 512],
                               psY[yb][:, half * 512:(half + 1) * 512],
                               stot_sb[:, 0:1],
                               fb_sb[:, half * 512:(half + 1) * 512],
                               ALU.mult, ALU.add),
                           reads=[r_psY[yb], r_stot, r_fb], writes=[r_ybf[i]])
                sch.op("vector", lambda e, i=i: e.tensor_copy(
                    ybf[i][:, :].bitcast(I32), ybf[i][:, :].bitcast(I32)),
                    reads=[r_ybf[i]], writes=[r_ybf[i]])
                sch.op("vector", lambda e, i=i: e.scalar_tensor_tensor(
                    x1t[i][:, :], ybf[i][:, :], 0.5, xt[i][:, :], ALU.mult, ALU.add),
                    reads=[r_ybf[i], r_xt[i]], writes=[r_x1t[i]])
                if t < NTQ:
                    sch.op("sync", lambda e, t=t, i=i: e.dma_start(
                        out=dx1[t * 128:(t + 1) * 128, :], in_=x1t[i][:, :]),
                        reads=[r_x1t[i]], writes=[r_dx1], is_dma=True)
                layer_norm(x1t[i][:, :], [r_x1t[i]], stt[i], r_stt[i],
                           rxf[i][:, :], [r_rxf[i]])

            def stage3(t):
                i = t % NB
                tb = 0
                sch.op("vector", lambda e, i=i: e.tensor_mul(
                    rp1[i][:, :], rxf[i][:, 0:HF], cost[i][:, :]),
                    reads=[r_rxf[i], r_cos[i]], writes=[r_rp1[i]])
                sch.op("gpsimd", lambda e, i=i: e.tensor_mul(
                    rp2[i][:, :], rxf[i][:, HF:D], sint[i][:, :]),
                    reads=[r_rxf[i], r_sin[i]], writes=[r_rp2[i]])
                sch.op("vector", lambda e, i=i: e.tensor_sub(
                    rp1[i][:, :], rp1[i][:, :], rp2[i][:, :]),
                    reads=[r_rp2[i]], writes=[r_rp1[i]])
                sch.op("gpsimd", lambda e, i=i: e.tensor_mul(
                    rp2[i][:, :], rxf[i][:, 0:HF], sint[i][:, :]),
                    reads=[r_rxf[i], r_sin[i]], writes=[r_rp2[i]])
                sch.op("vector", lambda e, i=i: e.scalar_tensor_tensor(
                    rxf[i][:, HF:D], rxf[i][:, HF:D], 1.0, cost[i][:, :],
                    ALU.mult, ALU.mult),
                    reads=[r_cos[i]], writes=[r_rxf[i]])
                sch.op("vector", lambda e, i=i: e.tensor_add(
                    rxf[i][:, HF:D], rxf[i][:, HF:D], rp2[i][:, :]),
                    reads=[r_rp2[i]], writes=[r_rxf[i]])
                sch.op("vector", lambda e, i=i: e.tensor_copy(
                    rxf[i][:, 0:HF], rp1[i][:, :]),
                    reads=[r_rp1[i]], writes=[r_rxf[i]])
                sch.op("vector", lambda e, i=i: e.tensor_copy(
                    rxh_t[i][:, :], rxf[i][:, :]),
                    reads=[r_rxf[i]], writes=[r_rxh[i]])
                sch.op("vector", lambda e, i=i: e.tensor_sub(
                    rxl_t[i][:, :], rxf[i][:, :], rxh_t[i][:, :]),
                    reads=[r_rxf[i], r_rxh[i]], writes=[r_rxl[i]])
                for c in range(8):
                    sch.op("tensor", lambda e, i=i, c=c, tb=tb: e.transpose(
                        psTh[tb][:, c * 128:(c + 1) * 128],
                        rxh_t[i][:, c * 128:(c + 1) * 128], idb_sb[:, :]),
                        reads=[r_rxh[i], r_idb], writes=[r_psTh[tb]])
                    sch.op("tensor", lambda e, i=i, c=c, tb=tb: e.transpose(
                        psTl[tb][:, c * 128:(c + 1) * 128],
                        rxl_t[i][:, c * 128:(c + 1) * 128], idb_sb[:, :]),
                        reads=[r_rxl[i], r_idb], writes=[r_psTl[tb]])
                sch.op("scalar", lambda e, i=i, tb=tb: e.copy(
                    rxhT[i][:, :], psTh[tb][:, :]),
                    reads=[r_psTh[tb]], writes=[r_rxhT[i]])
                sch.op("vector", lambda e, i=i, tb=tb: e.tensor_copy(
                    rxlT[i][:, :], psTl[tb][:, :]),
                    reads=[r_psTl[tb]], writes=[r_rxlT[i]])
                sch.op("sync", lambda e, t=t, i=i: e.dma_start(
                    out=drxh[:, :, t * 128:(t + 1) * 128].rearrange("c p q -> p c q"),
                    in_=rxhT[i][:, :]),
                    reads=[r_rxhT[i]], writes=[r_drx], is_dma=True)
                sch.op("gpsimd", lambda e, t=t, i=i: e.dma_start(
                    out=drxl[:, :, t * 128:(t + 1) * 128].rearrange("c p q -> p c q"),
                    in_=rxlT[i][:, :]),
                    reads=[r_rxlT[i]], writes=[r_drx], is_dma=True)
                wi = wsrc_list[t // 8][1]
                chunk = t % 8
                sch.op("vector", lambda e, i=i: e.tensor_copy(
                    wsh[i][:, :], wst[i][:, :]),
                    reads=[r_wst[i]], writes=[r_wsh[i]])
                sch.op("vector", lambda e, i=i: e.tensor_sub(
                    wsl[i][:, :], wst[i][:, :], wsh[i][:, :]),
                    reads=[r_wst[i], r_wsh[i]], writes=[r_wsl[i]])
                if wi < 3:
                    sch.op("gpsimd", lambda e, wi=wi, chunk=chunk, i=i: e.dma_start(
                        out=dwsp[wi, 0, chunk, :, :], in_=wsh[i][:, :]),
                        reads=[r_wsh[i]], writes=[r_dwsp], is_dma=True)
                    sch.op("gpsimd", lambda e, wi=wi, chunk=chunk, i=i: e.dma_start(
                        out=dwsp[wi, 1, chunk, :, :], in_=wsl[i][:, :]),
                        reads=[r_wsl[i]], writes=[r_dwsp], is_dma=True)
                else:
                    sch.op("gpsimd", lambda e, chunk=chunk, i=i: e.dma_start(
                        out=dwosp[0, chunk, :, :], in_=wsh[i][:, :]),
                        reads=[r_wsh[i]], writes=[r_dwosp], is_dma=True)
                    sch.op("gpsimd", lambda e, chunk=chunk, i=i: e.dma_start(
                        out=dwosp[1, chunk, :, :], in_=wsl[i][:, :]),
                        reads=[r_wsl[i]], writes=[r_dwosp], is_dma=True)

            for step in range(NT + 3):
                for s, fn in ((3, stage3), (2, stage2), (1, stage1), (0, stage0)):
                    tt = step - s
                    if 0 <= tt < NT:
                        fn(tt)

            if "x1" in tap_aps:
                sch.op("sync", lambda e: e.dma_start(out=tap_aps["x1"], in_=dx1),
                       reads=[r_dx1], writes=[r_out], is_dma=True)
            if "rxh" in tap_aps:
                sch.op("sync", lambda e: e.dma_start(out=tap_aps["rxh"], in_=drxh),
                       reads=[r_drx], writes=[r_out], is_dma=True)
            if "rxl" in tap_aps:
                sch.op("sync", lambda e: e.dma_start(out=tap_aps["rxl"], in_=drxl),
                       reads=[r_drx], writes=[r_out], is_dma=True)
            sch.barrier()

        if phases >= 2:
            with ExitStack() as phb:
                pb = phb.enter_context
                rxmh = pb(nc.sbuf_tensor("rxmh", [128, 8 * TQ], BF16))
                rxml = pb(nc.sbuf_tensor("rxml", [128, 8 * TQ], BF16))
                SBW = 256
                rxsh = [pb(nc.sbuf_tensor(f"rxsh{i}", [128, 8 * SBW], BF16))
                        for i in range(2)]
                rxsl = [pb(nc.sbuf_tensor(f"rxsl{i}", [128, 8 * SBW], BF16))
                        for i in range(2)]
                vgh = pb(nc.sbuf_tensor("vgh", [128, NKT * VROW], BF16))
                vgl = pb(nc.sbuf_tensor("vgl", [128, NKT * VROW], BF16))
                kst = [pb(nc.sbuf_tensor(f"kst{h}", [128, S], BF16)) for h in range(2)]
                kstg = pb(nc.sbuf_tensor("kstg", [128, S], BF16))
                qst = [pb(nc.sbuf_tensor(f"qst{h}", [128, TQ], BF16)) for h in range(2)]
                qlo = [pb(nc.sbuf_tensor(f"qlo{h}", [64, TQ], BF16))
                       for h in range(2)]
                qsg = kstg
                wph = pb(nc.sbuf_tensor("wph", [128, 8 * 128], BF16))
                wpl = pb(nc.sbuf_tensor("wpl", [128, 8 * 128], BF16))
                wqh = pb(nc.sbuf_tensor("wqh", [128, 8 * 128], BF16))
                wql = pb(nc.sbuf_tensor("wql", [128, 8 * 128], BF16))
                wvh_t = pb(nc.sbuf_tensor("wvh_t", [128, 8 * 256], BF16))
                wvl_t = pb(nc.sbuf_tensor("wvl_t", [128, 8 * 256], BF16))
                NPB = 4
                p32 = [pb(nc.sbuf_tensor(f"p32_{i}", [128, QB], F32))
                       for i in range(NPB)]
                phv = [p32[i][:, :].bitcast(BF16)[:, 1::2]
                       for i in range(NPB)]
                pstl = [pb(nc.sbuf_tensor(f"pstl{i}", [128, QB], BF16))
                        for i in range(NPB)]
                ascr = pb(nc.sbuf_tensor("ascr", [128, 4 * QB], F32))
                aun = ascr[0:64, 0:QB]
                ahi = ascr[0:64, QB:QB + QB // 2].bitcast(BF16)
                alo = ascr[0:64, QB + QB // 2:2 * QB].bitcast(BF16)
                rrows = ascr[0:2, 2 * QB:2 * QB + QB // 2].bitcast(BF16)
                gmax = ascr[0:1, 2 * QB + QB // 2:2 * QB + QB // 2 + 600]
                mxs = pb(nc.sbuf_tensor("mxs", [128, 48], F32))
                mxb = pb(nc.sbuf_tensor("mxb", [128, 4], BF16))
                psS = [pb(nc.psum_tensor(f"psS{i}", [128, QB], F32))
                        for i in range(4)]
                psKQ = psS
                psA = [pb(nc.psum_tensor(f"psA{i}", [128, QB], F32))
                       for i in range(2)]
                psRb = pb(nc.psum_tensor("psRb", [128, QB], F32))
                psRc = pb(nc.psum_tensor("psRc", [128, QB], F32))

                r_rxm = R("rxm")
                r_rxs = [R(), R()]
                r_vg = R("vg")
                r_kst = [R(), R()]
                r_kstg = R("kstg")
                r_qst = [R(), R()]
                r_qlo = [R(), R()]
                r_kasm = [R(), R()]
                r_qasm = [R(), R()]
                r_qsg = r_kstg
                r_wp = R("wp")
                r_wq = R("wq")
                r_wvh, r_wvl = R("wvh"), R("wvl")
                r_p32 = [R() for _ in range(NPB)]
                r_pstl = [R() for _ in range(NPB)]
                r_aun, r_rrows = R("aun"), R("rrows")
                r_ahi, r_alo, r_mxs, r_mxb = R("ahi"), R("alo"), R("mxs"), R("mxb")
                r_gmax = R("gmax")
                r_mxsg = R("mxsg")
                r_mxsa = [R() for _ in range(4)]
                r_g2, r_g3, r_g4, r_g5 = R(), R(), R(), R()
                r_psS = [R(f"psS{i}") for i in range(4)]
                r_psKQ = r_psS
                r_psA = [R("psA0"), R("psA1")]
                r_psRb = R("psRb")
                r_psRc = R("psRc")
                pending = [None]

                dln = ascr
                r_dln = R("dln")

                for c in range(8):
                    sch.op("sync", lambda e, c=c: e.dma_start(
                        out=rxmh[:, c * TQ:(c + 1) * TQ], in_=drxh[c, :, 0:TQ]),
                        reads=[r_drx], writes=[r_rxm], is_dma=True)
                    sch.op("sync", lambda e, c=c: e.dma_start(
                        out=rxml[:, c * TQ:(c + 1) * TQ], in_=drxl[c, :, 0:TQ]),
                        reads=[r_drx], writes=[r_rxm], is_dma=True)
                FLV = NKT * VROW
                sch.op("vector", lambda e: e.memset(
                    bass.AP(vgh, 64, [[FLV, 128], [VSL, NKT * VGH], [1, 1]]), 1.0),
                    writes=[r_vg])
                sch.op("vector", lambda e: e.memset(
                    bass.AP(vgl, 64, [[FLV, 128], [VSL, NKT * VGH], [1, 1]]), 0.0),
                    writes=[r_vg])

                def rx_slice(hilo, c, k0, width):
                    if k0 < TQ:
                        tns = rxmh if hilo == 0 else rxml
                        return (tns[:, c * TQ + k0: c * TQ + k0 + width], r_rxm)
                    blk = (k0 - TQ) // SBW
                    off = (k0 - TQ) % SBW
                    bi = blk % 2
                    tns = rxsh[bi] if hilo == 0 else rxsl[bi]
                    return (tns[:, c * SBW + off: c * SBW + off + width], r_rxs[bi])

                def load_block(blk):
                    bi = blk % 2
                    k0 = TQ + blk * SBW
                    sch.op("sync", lambda e, k0=k0, bi=bi: e.dma_start(
                        out=rxsh[bi][:, :],
                        in_=drxh[:, :, k0:k0 + SBW].rearrange("c p q -> p c q")),
                        reads=[r_drx], writes=[r_rxs[bi]], is_dma=True)
                    sch.op("sync", lambda e, k0=k0, bi=bi: e.dma_start(
                        out=rxsl[bi][:, :],
                        in_=drxl[:, :, k0:k0 + SBW].rearrange("c p q -> p c q")),
                        reads=[r_drx], writes=[r_rxs[bi]], is_dma=True)

                def gram_group(hs, src, r_src, g8, ntile, base_col):
                    bank = psRb if hs == 0 else psRc
                    r_bank = r_psRb if hs == 0 else r_psRc

                    def g_mms(e, g8=g8, src=src, ntile=ntile, bank=bank):
                        last = None
                        for t4 in range(4):
                            kt = g8 * 4 + t4
                            if kt >= ntile:
                                break
                            last = e.matmul(
                                bank[:, t4 * 128:(t4 + 1) * 128],
                                src[:, kt * 128:(kt + 1) * 128],
                                src[:, kt * 128:(kt + 1) * 128],
                                start=True, stop=True)
                        return last
                    sch.op("tensor", g_mms, reads=[r_src], writes=[r_bank])
                    nt4 = min(4, ntile - g8 * 4) * 128
                    sch.op("vector", lambda e, nt4=nt4, bank=bank,
                           col=base_col + g8: e.reduce_max(
                               mxs[:, col:col + 1], bank[:, 0:nt4],
                               axis=AXL.X,
                               apply_absolute_value=True),
                           reads=[r_bank], writes=[r_mxsg])

                def gram_final_both():
                    ngk = (NKT + 3) // 4
                    ngq = (NQB + 3) // 4
                    for k, (c0, n0) in enumerate(
                            ((16, ngk), (24, ngq), (32, ngk), (40, ngq))):
                        sch.op("vector", lambda e, k=k, c0=c0, n0=n0:
                               e.reduce_max(
                                   mxs[:, 1 + k:2 + k], mxs[:, c0:c0 + n0],
                                   axis=AXL.X),
                               reads=[r_mxsg], writes=[r_mxsa[k]])
                        sch.op("gpsimd", lambda e, k=k: e.dma_start(
                            out=gmax[0:1, k * 128:(k + 1) * 128],
                            in_=mxs[:, 1 + k:2 + k]),
                            reads=[r_mxsa[k]], writes=[r_gmax], is_dma=True)
                    sch.op("vector", lambda e: e.reduce_max(
                        gmax[0:1, 512:516],
                        gmax[0:1, 0:512].rearrange("p (f c) -> p f c", c=128),
                        axis=AXL.X),
                        reads=[r_gmax], writes=[r_g2])
                    sch.op("scalar", lambda e: e.activation(
                        gmax[0:1, 516:520], gmax[0:1, 512:516], AF.Ln),
                        reads=[r_g2], writes=[r_g3])
                    sch.op("vector", lambda e: e.tensor_add(
                        gmax[0:1, 520:522],
                        gmax[0:1, 516:520][:, 0::2],
                        gmax[0:1, 516:520][:, 1::2]),
                        reads=[r_g3], writes=[r_g4])
                    sch.op("scalar", lambda e: e.activation(
                        gmax[0:1, 522:524], gmax[0:1, 520:522],
                        AF.Exp, scale=0.5),
                        reads=[r_g4], writes=[r_g5])
                    sch.op("vector", lambda e: e.tensor_scalar(
                        mxb[0:1, 0:2], gmax[0:1, 522:524], -0.15,
                        None, ALU.mult),
                        reads=[r_g5], writes=[r_mxb])

                def shift_bcast(hs):
                    sch.op("tensor", lambda e, hs=hs: e.matmul(
                        psRc[:, 0:1], ones2[0:1, 0:128], mxb[0:1, hs:hs + 1],
                        start=True, stop=True),
                        reads=[r_mxb, r_ones2], writes=[r_psRc])
                    sch.op("scalar", lambda e, hs=hs: e.copy(
                        mxs[:, 9 + hs:10 + hs], psRc[:, 0:1]),
                        reads=[r_psRc], writes=[r_mxs])

                def attn_pair(pair):
                    LA = 4
                    total = NQB * NKT
                    tot2 = 2 * total

                    def emit_S(gi):
                        hd = gi // total
                        lk = gi % total
                        qb, kt = divmod(lk, NKT)
                        q0 = qb * QB
                        si = gi % NPB
                        bc = 9 + hd
                        sch.op("tensor", lambda e, kt=kt, q0=q0, si=si, hd=hd: (
                            e.matmul(psS[si][:, :],
                                     kst[hd][:, kt * 128:(kt + 1) * 128],
                                     qst[hd][:, q0:q0 + QB],
                                     start=True, stop=False),
                            e.matmul(psS[si][:, :],
                                     kst[hd][0:64, kt * 128:(kt + 1) * 128],
                                     qlo[hd][:, q0:q0 + QB],
                                     start=False, stop=True))[-1],
                            reads=[r_kst[hd], r_qst[hd], r_qlo[hd],
                                   r_kasm[hd], r_qasm[hd]],
                            writes=[r_psS[si]])
                        sch.op("scalar", lambda e, si=si, bc=bc: e.activation(
                            p32[si][:, :], psS[si][:, :], AF.Exp,
                            bias=mxs[:, bc:bc + 1], scale=0.125),
                            reads=[r_psS[si], r_mxs], writes=[r_p32[si]])
                        sch.op("vector", lambda e, si=si: e.tensor_sub(
                            pstl[si][:, :], p32[si][:, :], phv[si]),
                            reads=[r_p32[si]], writes=[r_pstl[si]])

                    def emit_AV(gj):
                        hd = gj // total
                        lk = gj % total
                        qb, kt = divmod(lk, NKT)
                        sj = gj % NPB
                        ab = (gj // NKT) % 2
                        vc = ((pair * 2 + hd) % VGH) * VSL
                        c0 = kt * VROW + vc
                        sch.op("tensor", lambda e, c0=c0, sj=sj, kt=kt, ab=ab: (
                            e.matmul(psA[ab][0:VSL, :], vgh[:, c0:c0 + VSL],
                                     phv[sj], start=(kt == 0), stop=False),
                            e.matmul(psA[ab][0:VSL, :], vgl[:, c0:c0 + VSL],
                                     phv[sj], start=False, stop=False),
                            e.matmul(psA[ab][0:VSL, :], vgh[:, c0:c0 + VSL],
                                     pstl[sj][:, :], start=False,
                                     stop=(kt == NKT - 1)))[-1],
                            reads=[r_vg, r_p32[sj], r_pstl[sj]],
                            writes=[r_psA[ab]])

                    def finish_evac(gqb):
                        ab = gqb % 2
                        sch.op("scalar", lambda e, ab=ab: e.copy(
                            aun[:, :], psA[ab][0:64, :]),
                            reads=[r_psA[ab]], writes=[r_aun])
                        sch.op("scalar", lambda e, ab=ab: e.copy(
                            dln[64:65, 0:QB], psA[ab][64:65, :]),
                            reads=[r_psA[ab]], writes=[r_dln])
                        sch.op("scalar", lambda e: e.activation(
                            dln[64:65, QB:2 * QB], dln[64:65, 0:QB], AF.Ln),
                            reads=[r_dln], writes=[r_dln])
                        sch.op("scalar", lambda e: e.activation(
                            dln[64:65, QB:2 * QB], dln[64:65, QB:2 * QB],
                            AF.Exp, scale=-1.0),
                            reads=[r_dln], writes=[r_dln])
                        sch.op("vector", lambda e: e.tensor_mul(
                            dln[64:65, 2 * QB:3 * QB], dln[64:65, 0:QB],
                            dln[64:65, QB:2 * QB]),
                            reads=[r_dln], writes=[r_dln])
                        sch.op("vector", lambda e: e.tensor_scalar(
                            dln[64:65, 2 * QB:3 * QB], dln[64:65, 2 * QB:3 * QB],
                            -1.0, 2.0, ALU.mult, ALU.add),
                            reads=[r_dln], writes=[r_dln])
                        sch.op("vector", lambda e: e.tensor_mul(
                            dln[64:65, QB:2 * QB], dln[64:65, QB:2 * QB],
                            dln[64:65, 2 * QB:3 * QB]),
                            reads=[r_dln], writes=[r_dln])
                        sch.op("vector", lambda e: e.tensor_copy(
                            dln[64:65, 3 * QB:3 * QB + QB // 2].bitcast(BF16),
                            dln[64:65, QB:2 * QB]),
                            reads=[r_dln], writes=[r_dln])
                        sch.op("vector", lambda e: e.tensor_sub(
                            dln[64:65, 3 * QB + QB // 2:4 * QB].bitcast(BF16),
                            dln[64:65, QB:2 * QB],
                            dln[64:65, 3 * QB:3 * QB + QB // 2].bitcast(BF16)),
                            reads=[r_dln], writes=[r_dln])
                        sch.op("gpsimd", lambda e: e.dma_start(
                            out=rrows[0:1, :],
                            in_=dln[64:65, 3 * QB:3 * QB + QB // 2].bitcast(BF16)),
                            reads=[r_dln], writes=[r_rrows], is_dma=True)
                        sch.op("gpsimd", lambda e: e.dma_start(
                            out=rrows[1:2, :],
                            in_=dln[64:65, 3 * QB + QB // 2:4 * QB].bitcast(BF16)),
                            reads=[r_dln], writes=[r_rrows], is_dma=True)

                    def finish_norm(gqb, bank=None, r_bank=None):
                        if bank is None:
                            bank, r_bank = psRb, r_psRb
                        h = pair * 2 + gqb // NQB
                        q0 = (gqb % NQB) * QB
                        sch.op("tensor", lambda e, bank=bank: e.matmul(
                            bank[0:64, :], ones2[0:2, 0:64], rrows[0:2, :],
                            start=True, stop=True),
                            reads=[r_rrows, r_ones2], writes=[r_bank])
                        sch.op("vector", lambda e, bank=bank: e.tensor_mul(
                            aun[:, :], aun[:, :], bank[0:64, :]),
                            reads=[r_bank], writes=[r_aun])
                        sch.op("scalar", lambda e: e.copy(ahi[:, :], aun[:, :]),
                               reads=[r_aun], writes=[r_ahi])
                        sch.op("vector", lambda e: e.tensor_sub(
                            alo[:, :], aun[:, :], ahi[:, :]),
                            reads=[r_aun, r_ahi], writes=[r_alo])
                        sch.op("gpsimd", lambda e, h=h, q0=q0: e.dma_start(
                            out=da[0, h * 64:(h + 1) * 64, q0:q0 + QB],
                            in_=ahi[:, :]),
                            reads=[r_ahi], writes=[r_da], is_dma=True)
                        sch.op("gpsimd", lambda e, h=h, q0=q0: e.dma_start(
                            out=da[1, h * 64:(h + 1) * 64, q0:q0 + QB],
                            in_=alo[:, :]),
                            reads=[r_alo], writes=[r_da], is_dma=True)

                    def warm_fill(nmm):
                        def mms(e, nmm=nmm):
                            last = None
                            for _ in range(nmm):
                                last = e.matmul(
                                    psRc[:, :], kst[0][:, 0:128],
                                    qst[0][:, 0:QB], start=True, stop=True)
                            return last
                        sch.op("tensor", mms,
                               reads=[r_kst[0], r_qst[0], r_kasm[0],
                                      r_qasm[0]],
                               writes=[r_psRc])

                    for gi in range(tot2 + LA):
                        if gi == 0:
                            shift_bcast(0)
                        if gi == total - 2:
                            shift_bcast(1)
                        if gi == 6 and pending[0] is not None:
                            pending[0]()
                            pending[0] = None
                        if gi == 4:
                            warm_fill(3)
                        elif gi == 5:
                            warm_fill(2)
                        if gi >= LA:
                            gj = gi - LA
                            kt = gj % NKT
                            gqb = gj // NKT
                            emit_AV(gj)
                            if kt == NKT - 1:
                                finish_evac(gqb)
                            if kt == NKT // 2 and gqb > 0:
                                finish_norm(gqb - 1)
                        if gi < tot2:
                            emit_S(gi)
                    gq_last = 2 * NQB - 1
                    pending[0] = (lambda fn=finish_norm: fn(
                        gq_last, psA[gq_last % 2], r_psA[gq_last % 2]))

                NOB = TQ // SBW

                def load_vw(g):
                    gf0 = g * VGH * HD
                    sch.op("gpsimd", lambda e, gf0=gf0: e.dma_start(
                        out=wvh_t[:, :],
                        in_=dwsp[2, 0, :, :, gf0:gf0 + 256]
                        .rearrange("c p q -> p c q")),
                        reads=[r_dwsp], writes=[r_wvh], is_dma=True)
                    sch.op("gpsimd", lambda e, gf0=gf0: e.dma_start(
                        out=wvl_t[:, :],
                        in_=dwsp[2, 1, :, :, gf0:gf0 + 256]
                        .rearrange("c p q -> p c q")),
                        reads=[r_dwsp], writes=[r_wvl], is_dma=True)

                def load_qw(pair):
                    pf0 = pair * 128
                    sch.op("gpsimd", lambda e, pf0=pf0: e.dma_start(
                        out=wqh[:, :],
                        in_=dwsp[0, 0, :, :, pf0:pf0 + 128]
                        .rearrange("c p q -> p c q")),
                        reads=[r_dwsp], writes=[r_wq], is_dma=True)
                    sch.op("gpsimd", lambda e, pf0=pf0: e.dma_start(
                        out=wql[:, :],
                        in_=dwsp[0, 1, :, :, pf0:pf0 + 128]
                        .rearrange("c p q -> p c q")),
                        reads=[r_dwsp], writes=[r_wq], is_dma=True)

                def load_pw(kind, pair):
                    pf0 = pair * 128
                    sch.op("gpsimd", lambda e, pf0=pf0, kind=kind: e.dma_start(
                        out=wph[:, :],
                        in_=dwsp[kind, 0, :, :, pf0:pf0 + 128]
                        .rearrange("c p q -> p c q")),
                        reads=[r_dwsp], writes=[r_wp], is_dma=True)
                    sch.op("gpsimd", lambda e, pf0=pf0, kind=kind: e.dma_start(
                        out=wpl[:, :],
                        in_=dwsp[kind, 1, :, :, pf0:pf0 + 128]
                        .rearrange("c p q -> p c q")),
                        reads=[r_dwsp], writes=[r_wp], is_dma=True)

                load_vw(0)
                load_pw(1, 0)
                load_qw(0)
                for g in range(H // VGH):
                    gf0 = g * VGH * HD
                    for kt in range(NKT):
                        k0 = kt * 128
                        if kt == 4:
                            load_block(0)
                        elif kt == 8:
                            load_block(1)
                        elif kt >= 18 and kt % 2 == 0 and (kt - 18) // 2 + 2 < NOB:
                            load_block((kt - 18) // 2 + 2)
                        si = kt % 2
                        rsrc = rx_slice(0, 0, k0, 128)[1]

                        def v_mms(e, k0=k0, si=si, gf0=gf0):
                            last = None
                            for c in range(8):
                                rxh_ap, _ = rx_slice(0, c, k0, 128)
                                rxl_ap, _ = rx_slice(1, c, k0, 128)
                                last = e.matmul(psKQ[si][:, 0:256], rxh_ap,
                                                wvh_t[:, c * 256:(c + 1) * 256],
                                                start=(c == 0), stop=False)
                                last = e.matmul(psKQ[si][:, 0:256], rxh_ap,
                                                wvl_t[:, c * 256:(c + 1) * 256],
                                                start=False, stop=False)
                                last = e.matmul(psKQ[si][:, 0:256], rxl_ap,
                                                wvh_t[:, c * 256:(c + 1) * 256],
                                                start=False, stop=False)
                            last = e.matmul(psKQ[si][:, 0:256], ones2[0:2, 0:128],
                                            bvrows[0:2, gf0:gf0 + 256],
                                            start=False, stop=True)
                            return last
                        sch.op("tensor", v_mms,
                               reads=[r_wvh, r_wvl, r_bvrows, r_ones2, rsrc],
                               writes=[r_psKQ[si]])
                        sch.op("scalar", lambda e, kt=kt, si=si: e.copy(
                            bass.AP(vgh, kt * VROW, [[FLV, 128], [VSL, VGH], [1, HD]]),
                            psKQ[si][:, 0:256]),
                            reads=[r_psKQ[si]], writes=[r_vg])
                        sch.op("vector", lambda e, kt=kt, si=si: e.scalar_tensor_tensor(
                            bass.AP(vgl, kt * VROW, [[FLV, 128], [VSL, VGH], [1, HD]]),
                            psKQ[si][:, 0:256], 1.0,
                            bass.AP(vgh, kt * VROW, [[FLV, 128], [VSL, VGH], [1, HD]]),
                            ALU.mult, ALU.subtract),
                            reads=[r_psKQ[si], r_vg], writes=[r_vg])

                    for pi in range(2):
                        pair = g * 2 + pi
                        pf0 = pair * 128
                        NCH = S // 256
                        for n in range(NCH):
                            k0 = n * 256
                            if n == 2 and pending[0] is not None:
                                pending[0]()
                                pending[0] = None
                            if n == 8:
                                load_block(0)
                                load_block(1)
                            elif 9 <= n <= NCH - 2:
                                load_block(n - 7)
                            si = n % 2
                            rsrc = rx_slice(0, 0, k0, 256)[1]

                            def k_mms(e, k0=k0, si=si):
                                last = None
                                for wt, hl in ((wph, 0), (wpl, 0), (wph, 1)):
                                    for c in range(8):
                                        rx_ap, _ = rx_slice(hl, c, k0, 256)
                                        first = (wt is wph and hl == 0 and c == 0)
                                        lastf = (wt is wph and hl == 1 and c == 7)
                                        last = e.matmul(
                                            psKQ[si][:, 0:256],
                                            wt[:, c * 128:(c + 1) * 128],
                                            rx_ap, start=first, stop=lastf)
                                return last
                            sch.op("tensor", k_mms, reads=[r_wp, rsrc],
                                   writes=[r_psKQ[si]])
                            nn = n
                            sch.op("scalar", lambda e, nn=nn, si=si, pair=pair:
                                   e.activation(
                                       kst[0][0:64, nn * 256:(nn + 1) * 256],
                                       psKQ[si][0:64, 0:256], AF.Identity,
                                       bias=bk_sb[0:64, pair:pair + 1]),
                                   reads=[r_psKQ[si], r_bk], writes=[r_kst[0]])
                            sch.op("vector", lambda e, nn=nn, si=si, pair=pair:
                                   e.scalar_tensor_tensor(
                                       kstg[0:64, nn * 256:(nn + 1) * 256],
                                       psKQ[si][0:64, 0:256],
                                       bk_sb[0:64, pair:pair + 1],
                                       kst[0][0:64, nn * 256:(nn + 1) * 256],
                                       ALU.add, ALU.subtract),
                                   reads=[r_psKQ[si], r_bk, r_kst[0]],
                                   writes=[r_kstg])
                            sch.op("scalar", lambda e, nn=nn, si=si, pair=pair:
                                   e.activation(
                                       kstg[64:128, nn * 256:(nn + 1) * 256],
                                       psKQ[si][64:128, 0:256], AF.Identity,
                                       bias=bk_sb[64:128, pair:pair + 1]),
                                   reads=[r_psKQ[si], r_bk], writes=[r_kstg])
                            sch.op("vector", lambda e, nn=nn, si=si, pair=pair:
                                   e.scalar_tensor_tensor(
                                       kst[1][64:128, nn * 256:(nn + 1) * 256],
                                       psKQ[si][64:128, 0:256],
                                       bk_sb[64:128, pair:pair + 1],
                                       kstg[64:128, nn * 256:(nn + 1) * 256],
                                       ALU.add, ALU.subtract),
                                   reads=[r_psKQ[si], r_bk, r_kstg],
                                   writes=[r_kst[1]])
                            if n % 2 == 1:
                                g8 = n // 2
                                gram_group(0, kst[0][0:64, :], r_kst[0],
                                           g8, NKT, 16)
                                gram_group(1, kstg[64:128, :], r_kstg,
                                           g8, NKT, 32)
                        sch.op("gpsimd", lambda e: e.dma_start(
                            out=kst[0][64:128, :], in_=kstg[0:64, :]),
                            reads=[r_kstg], writes=[r_kasm[0]], is_dma=True)
                        sch.op("gpsimd", lambda e: e.dma_start(
                            out=kst[1][0:64, :], in_=kstg[64:128, :]),
                            reads=[r_kstg], writes=[r_kasm[1]], is_dma=True)

                        for n in range(4):
                            q0 = n * 512
                            si = n % 2

                            def q_mms(e, q0=q0, si=si):
                                last = None
                                for wt, hl in ((wqh, 0), (wql, 0), (wqh, 1)):
                                    for c in range(8):
                                        rx_ap, _ = rx_slice(hl, c, q0, 512)
                                        first = (wt is wqh and hl == 0 and c == 0)
                                        lastf = (wt is wqh and hl == 1 and c == 7)
                                        last = e.matmul(
                                            psKQ[si][:, :],
                                            wt[:, c * 128:(c + 1) * 128],
                                            rx_ap, start=first, stop=lastf)
                                return last
                            sch.op("tensor", q_mms, reads=[r_wq, r_rxm],
                                   writes=[r_psKQ[si]])
                            nn = n
                            sch.op("scalar", lambda e, nn=nn, si=si, pair=pair:
                                   e.activation(
                                       qst[0][0:64, nn * 512:(nn + 1) * 512],
                                       psKQ[si][0:64, :], AF.Identity,
                                       bias=bq_sb[0:64, pair:pair + 1]),
                                   reads=[r_psKQ[si], r_bq], writes=[r_qst[0]])
                            sch.op("vector", lambda e, nn=nn, si=si, pair=pair:
                                   e.scalar_tensor_tensor(
                                       qlo[0][:, nn * 512:(nn + 1) * 512],
                                       psKQ[si][0:64, :],
                                       bq_sb[0:64, pair:pair + 1],
                                       qst[0][0:64, nn * 512:(nn + 1) * 512],
                                       ALU.add, ALU.subtract),
                                   reads=[r_psKQ[si], r_bq, r_qst[0]],
                                   writes=[r_qlo[0]])
                            sch.op("scalar", lambda e, nn=nn, si=si, pair=pair:
                                   e.activation(
                                       qst[1][64:128, nn * 512:(nn + 1) * 512],
                                       psKQ[si][64:128, :], AF.Identity,
                                       bias=bq_sb[64:128, pair:pair + 1]),
                                   reads=[r_psKQ[si], r_bq], writes=[r_qst[1]])
                            sch.op("vector", lambda e, nn=nn, si=si, pair=pair:
                                   e.scalar_tensor_tensor(
                                       qsg[64:128, nn * 512:(nn + 1) * 512],
                                       psKQ[si][64:128, :],
                                       bq_sb[64:128, pair:pair + 1],
                                       qst[1][64:128, nn * 512:(nn + 1) * 512],
                                       ALU.add, ALU.subtract),
                                   reads=[r_psKQ[si], r_bq, r_qst[1]],
                                   writes=[r_qsg])
                            if n == 1:
                                gram_group(0, qst[0][0:64, :], r_qst[0],
                                           0, NQB, 24)
                                gram_group(1, qst[1][64:128, :], r_qst[1],
                                           0, NQB, 40)
                                gram_final_both()
                        sch.op("gpsimd", lambda e: e.dma_start(
                            out=qst[0][64:128, :], in_=qst[0][0:64, :]),
                            reads=[r_qst[0]], writes=[r_qasm[0]], is_dma=True)
                        sch.op("gpsimd", lambda e: e.dma_start(
                            out=qst[1][0:64, :], in_=qst[1][64:128, :]),
                            reads=[r_qst[1]], writes=[r_qasm[1]], is_dma=True)
                        sch.op("gpsimd", lambda e: e.dma_start(
                            out=qlo[1][:, :], in_=qsg[64:128, 0:TQ]),
                            reads=[r_qsg], writes=[r_qlo[1]], is_dma=True)

                        if pair < 7:
                            load_pw(1, pair + 1)
                            load_qw(pair + 1)
                        if pi == 1 and g < H // VGH - 1:
                            load_vw(g + 1)
                        attn_pair(pair)

                if pending[0] is not None:
                    pending[0]()
                    pending[0] = None
                if "a" in tap_aps:
                    sch.op("sync", lambda e: e.dma_start(out=tap_aps["a"], in_=da),
                           reads=[r_da], writes=[r_out], is_dma=True)
                sch.barrier()

        if phases >= 3:
            with ExitStack() as phc:
                pc = phc.enter_context
                ahs = pc(nc.sbuf_tensor("ahs", [128, 8 * TQ], BF16))
                als = pc(nc.sbuf_tensor("als", [128, 8 * TQ], BF16))
                woh = pc(nc.sbuf_tensor("woh", [128, 8 * D], BF16))
                wol = pc(nc.sbuf_tensor("wol", [128, 8 * D], BF16))
                NC = 4
                x1c = [pc(nc.sbuf_tensor(f"x1c{i}", [128, D], F32)) for i in range(NC)]
                x2c = [pc(nc.sbuf_tensor(f"x2c{i}", [128, D], F32)) for i in range(NC)]
                xb3 = [pc(nc.sbuf_tensor(f"xb3_{i}", [128, D], BF16)) for i in range(NC)]
                xqT3 = [pc(nc.sbuf_tensor(f"xqT3_{i}", [128, D], FP8)) for i in range(NC)]
                yb3 = [pc(nc.sbuf_tensor(f"yb3_{i}", [128, D], BF16)) for i in range(NC)]
                oout = [pc(nc.sbuf_tensor(f"oout{i}", [128, D], F32)) for i in range(NC)]
                st3 = [pc(nc.sbuf_tensor(f"st3_{i}", [128, 12], F32)) for i in range(NC)]
                psO = [pc(nc.psum_tensor(f"psO{i}", [128, D], F32))
                       for i in range(2)]
                psT3 = [pc(nc.psum_tensor(f"psT3_{i}", [128, D], BF16))
                        for i in range(2)]
                psY3 = pc(nc.psum_tensor("psY3", [128, D], F32))

                r_ahs, r_als, r_woh, r_wol = R(), R(), R(), R()
                r_x1c = [R() for _ in range(NC)]
                r_x2c = [R() for _ in range(NC)]
                r_xb3 = [R() for _ in range(NC)]
                r_xqT3 = [R() for _ in range(NC)]
                r_yb3 = [R() for _ in range(NC)]
                r_oout = [R() for _ in range(NC)]
                r_st3 = [R() for _ in range(NC)]
                r_psO = [R(), R()]
                r_psT3 = [R(), R()]
                r_psY3 = R("psY3")

                for c in range(8):
                    sch.op("sync", lambda e, c=c: e.dma_start(
                        out=ahs[:, c * TQ:(c + 1) * TQ],
                        in_=da[0, c * 128:(c + 1) * 128, :]),
                        reads=[r_da], writes=[r_ahs], is_dma=True)
                    sch.op("sync", lambda e, c=c: e.dma_start(
                        out=als[:, c * TQ:(c + 1) * TQ],
                        in_=da[1, c * 128:(c + 1) * 128, :]),
                        reads=[r_da], writes=[r_als], is_dma=True)
                    sch.op("sync", lambda e, c=c: e.dma_start(
                        out=woh[:, c * D:(c + 1) * D], in_=dwosp[0, c, :, :]),
                        reads=[r_dwosp], writes=[r_woh], is_dma=True)
                    sch.op("sync", lambda e, c=c: e.dma_start(
                        out=wol[:, c * D:(c + 1) * D], in_=dwosp[1, c, :, :]),
                        reads=[r_dwosp], writes=[r_wol], is_dma=True)

                def layer_norm3(src_ap, r_src, st, r_st, dst_ap, r_dst):
                    sch.op("vector", lambda e: e.reduce_sum(
                        st[:, 0:1], src_ap, axis=AXL.X),
                        reads=[r_src], writes=[r_st])
                    sch.op("vector", lambda e: e.scalar_tensor_tensor(
                        ln_dump[:, :], src_ap, 1.0, src_ap, ALU.mult, ALU.mult,
                        accum_out=st[:, 1:2]),
                        reads=[r_src], writes=[r_dump, r_st])
                    sch.op("vector", lambda e: e.tensor_scalar(
                        st[:, 2:3], st[:, 0:1], -1.0 / D, None, ALU.mult),
                        reads=[r_st], writes=[r_st])
                    sch.op("vector", lambda e: e.scalar_tensor_tensor(
                        st[:, 3:4], st[:, 2:3], 1.0, st[:, 2:3], ALU.mult,
                        ALU.mult),
                        reads=[r_st], writes=[r_st])
                    sch.op("vector", lambda e: e.scalar_tensor_tensor(
                        st[:, 4:5], st[:, 3:4], -D, st[:, 1:2], ALU.mult,
                        ALU.add),
                        reads=[r_st], writes=[r_st])
                    sch.op("vector", lambda e: e.tensor_scalar(
                        st[:, 4:5], st[:, 4:5], 1.0 / D, None, ALU.mult),
                        reads=[r_st], writes=[r_st])
                    sch.op("vector", lambda e: e.tensor_scalar(
                        st[:, 9:10], st[:, 4:5], EPS, None, ALU.add),
                        reads=[r_st], writes=[r_st])
                    sch.op("scalar", lambda e: e.activation(
                        st[:, 5:6], st[:, 9:10], AF.Ln),
                        reads=[r_st], writes=[r_st])
                    sch.op("scalar", lambda e: e.activation(
                        st[:, 6:7], st[:, 5:6], AF.Exp, scale=-0.5),
                        reads=[r_st], writes=[r_st])
                    sch.op("vector", lambda e: e.scalar_tensor_tensor(
                        st[:, 7:8], st[:, 6:7], 1.0, st[:, 6:7], ALU.mult,
                        ALU.mult),
                        reads=[r_st], writes=[r_st])
                    sch.op("vector", lambda e: e.scalar_tensor_tensor(
                        st[:, 8:9], st[:, 9:10], 1.0, st[:, 7:8], ALU.mult,
                        ALU.mult),
                        reads=[r_st], writes=[r_st])
                    sch.op("vector", lambda e: e.tensor_scalar(
                        st[:, 8:9], st[:, 8:9], -0.5, 1.5, ALU.mult, ALU.add),
                        reads=[r_st], writes=[r_st])
                    sch.op("vector", lambda e: e.scalar_tensor_tensor(
                        st[:, 6:7], st[:, 8:9], 1.0, st[:, 6:7], ALU.mult,
                        ALU.mult),
                        reads=[r_st], writes=[r_st])
                    sch.op("vector", lambda e: e.tensor_scalar(
                        dst_ap, src_ap, st[:, 2:3], st[:, 6:7], ALU.add,
                        ALU.mult),
                        reads=[r_src, r_st], writes=[r_dst])

                def cstage0(t):
                    i = t % NC
                    sch.op("sync", lambda e, t=t, i=i: e.dma_start(
                        out=x1c[i][:, :], in_=dx1[t * 128:(t + 1) * 128, :]),
                        reads=[r_dx1], writes=[r_x1c[i]], is_dma=True)

                def cstage1(t):
                    i = t % NC
                    ob = t % 2
                    def o_mms(e, t=t, ob=ob):
                        last = None
                        for half in range(2):
                            o0 = half * 512
                            for ws, hl in ((woh, 0), (wol, 0), (woh, 1)):
                                for c in range(8):
                                    a_ap = (ahs if hl == 0 else als)[
                                        :, c * TQ + t * 128: c * TQ + (t + 1) * 128]
                                    first = (ws is woh and hl == 0 and c == 0)
                                    last = e.matmul(
                                        psO[ob][:, o0:o0 + 512], a_ap,
                                        ws[:, c * D + o0: c * D + o0 + 512],
                                        start=first, stop=False)
                            last = e.matmul(psO[ob][:, o0:o0 + 512],
                                            ones2[0:2, 0:128],
                                            borows[0:2, o0:o0 + 512],
                                            start=False, stop=True)
                        return last
                    sch.op("tensor", o_mms,
                           reads=[r_ahs, r_als, r_woh, r_wol, r_ones2, r_borows],
                           writes=[r_psO[ob]])
                    sch.op("vector", lambda e, i=i, ob=ob: e.tensor_add(
                        x2c[i][:, :], x1c[i][:, :], psO[ob][:, :]),
                        reads=[r_x1c[i], r_psO[ob]], writes=[r_x2c[i]])

                def cstage2(t):
                    i = t % NC
                    tb3 = t % 2
                    layer_norm3(x2c[i][:, :], r_x2c[i], st3[i], r_st3[i],
                                xb3[i][:, :], r_xb3[i])
                    for c in range(8):
                        sch.op("tensor", lambda e, i=i, c=c, tb3=tb3: e.transpose(
                            psT3[tb3][:, c * 128:(c + 1) * 128],
                            xb3[i][:, c * 128:(c + 1) * 128], idb_sb[:, :]),
                            reads=[r_xb3[i], r_idb], writes=[r_psT3[tb3]])
                    sch.op("vector", lambda e, i=i, tb3=tb3: e.tensor_scalar(
                        xqT3[i][:, :], psT3[tb3][:, 0:D], sinv_sb[:, 0:1], None,
                        ALU.mult),
                        reads=[r_psT3[tb3], r_sinv], writes=[r_xqT3[i]])
                    sch.op("vector", lambda e, i=i: e.tensor_copy(
                        xqT3[i][:, :].bitcast(I32), xqT3[i][:, :].bitcast(I32)),
                        reads=[r_xqT3[i]], writes=[r_xqT3[i]])

                def cstage3(t):
                    i = t % NC
                    for half in range(2):
                        for c in range(8):
                            sch.op("tensor", lambda e, i=i, c=c, half=half: e.matmul(
                                psY3[:, half * 512:(half + 1) * 512],
                                xqT3[i][:, c * 128:(c + 1) * 128],
                                kern8[:, c * D + half * 512:c * D + half * 512 + 512],
                                start=(c == 0), stop=(c == 7)),
                                reads=[r_xqT3[i], r_kern8], writes=[r_psY3])
                    for half in range(2):
                        sch.op("vector", lambda e, i=i, half=half:
                               e.scalar_tensor_tensor(
                                   yb3[i][:, half * 512:(half + 1) * 512],
                                   psY3[:, half * 512:(half + 1) * 512],
                                   stot_sb[:, 0:1],
                                   fb_sb[:, half * 512:(half + 1) * 512],
                                   ALU.mult, ALU.add),
                               reads=[r_psY3, r_stot, r_fb],
                               writes=[r_yb3[i]])
                    sch.op("vector", lambda e, i=i: e.tensor_copy(
                        yb3[i][:, :].bitcast(I32), yb3[i][:, :].bitcast(I32)),
                        reads=[r_yb3[i]], writes=[r_yb3[i]])
                    sch.op("vector", lambda e, i=i: e.scalar_tensor_tensor(
                        oout[i][:, :], yb3[i][:, :], 0.5, x2c[i][:, :],
                        ALU.mult, ALU.add),
                        reads=[r_yb3[i], r_x2c[i]], writes=[r_oout[i]])
                    sch.op("sync", lambda e, t=t, i=i: e.dma_start(
                        out=dout[t * 128:(t + 1) * 128, :], in_=oout[i][:, :]),
                        reads=[r_oout[i]], writes=[r_out], is_dma=True)

                for step in range(NTQ + 3):
                    for s, fn in ((3, cstage3), (2, cstage2), (1, cstage1),
                                  (0, cstage0)):
                        tt = step - s
                        if 0 <= tt < NTQ:
                            fn(tt)

        sch.op("sync", lambda e: e.nop(), reads=[r_out])

        with nc.Block() as block:
            sch.emit(nc, block)

    return nc, tap_aps


_ROPE_CACHE = {}


def _rope_tables():
    if "t" not in _ROPE_CACHE:
        half = D // 2
        freq = np.exp(np.arange(0, D, 2, dtype=np.float64) *
                      (-math.log(10000.0) / D))
        ang = np.arange(S, dtype=np.float64)[:, None] * freq[None, :]
        _ROPE_CACHE["t"] = (np.cos(ang).astype(np.float32),
                            np.sin(ang).astype(np.float32))
    return _ROPE_CACHE["t"]


_PROG_CACHE = {}


def _get_prog():
    if "p" not in _PROG_CACHE:
        _PROG_CACHE["p"] = build_program()[0]
    return _PROG_CACHE["p"]


def kernel(L0_attn_x, L0_attn_Wq, L0_attn_bq, L0_attn_Wk, L0_attn_bk,
           L0_attn_Wv, L0_attn_bv, L0_attn_Wo, L0_attn_bo, L0_ffn_kernel,
           L0_ffn_input_scale, L0_ffn_kernel_scale, L0_ffn_bias):
    x = np.ascontiguousarray(np.asarray(L0_attn_x, dtype=np.float32))
    wqt = np.ascontiguousarray(np.asarray(L0_attn_Wq, dtype=np.float32).T)
    wkt = np.ascontiguousarray(np.asarray(L0_attn_Wk, dtype=np.float32).T)
    wvt = np.ascontiguousarray(np.asarray(L0_attn_Wv, dtype=np.float32).T)
    wot = np.ascontiguousarray(np.asarray(L0_attn_Wo, dtype=np.float32).T)
    kern = np.asarray(L0_ffn_kernel)
    if kern.dtype != ml_dtypes.bfloat16:
        kern = kern.view(ml_dtypes.bfloat16) if kern.dtype == np.uint16 else \
            kern.astype(ml_dtypes.bfloat16)
    kern = np.ascontiguousarray(kern)
    bq = np.asarray(L0_attn_bq, dtype=np.float32)
    bk = np.asarray(L0_attn_bk, dtype=np.float32)
    bv = np.asarray(L0_attn_bv, dtype=np.float32)
    bo = np.asarray(L0_attn_bo, dtype=np.float32)
    fbias = np.asarray(L0_ffn_bias)
    if fbias.dtype != ml_dtypes.bfloat16:
        fbias = fbias.view(ml_dtypes.bfloat16) if fbias.dtype == np.uint16 else \
            fbias.astype(ml_dtypes.bfloat16)
    s_in = np.asarray(L0_ffn_input_scale, dtype=np.float32).reshape(1)
    s_k = np.asarray(L0_ffn_kernel_scale, dtype=np.float32).reshape(1)

    cos_t, sin_t = _rope_tables()
    bqp = np.ascontiguousarray(bq.reshape(8, 128).T)
    bkp = np.ascontiguousarray(bk.reshape(8, 128).T)
    bvr = np.ascontiguousarray(bv.reshape(1, D))
    bor = np.ascontiguousarray(bo.reshape(1, D))
    fb_t = np.ascontiguousarray(np.tile(fbias.reshape(1, D), (128, 1)))
    id8 = np.eye(128, dtype=ml_dtypes.float8_e4m3)
    idb = np.eye(128, dtype=ml_dtypes.bfloat16)
    s_in_bc = np.ascontiguousarray(np.tile(s_in.reshape(1, 1), (128, 1)))
    s_k_bc = np.ascontiguousarray(np.tile(s_k.reshape(1, 1), (128, 1)))

    in_maps = []
    for c in range(8):
        b = c // 2
        j = c % 2
        perm = np.r_[j * TQ:(j + 1) * TQ, (1 - j) * TQ:(2 - j) * TQ]
        in_maps.append({
            "x": np.ascontiguousarray(x[b][perm]),
            "wqt": wqt, "wkt": wkt, "wvt": wvt, "wot": wot,
            "kern": kern, "bqp": bqp, "bkp": bkp, "bvr": bvr, "bor": bor,
            "fbias": fb_t,
            "cos_t": np.ascontiguousarray(cos_t[perm]),
            "sin_t": np.ascontiguousarray(sin_t[perm]),
            "id8": id8, "idb": idb, "s_in": s_in_bc, "s_k": s_k_bc,
        })

    nc = _get_prog()
    res = run_bass_kernel_spmd(nc, in_maps, core_ids=list(range(8)))
    out = np.empty((B, S, D), dtype=np.float32)
    for c in range(8):
        b, j = c // 2, c % 2
        out[b, j * TQ:(j + 1) * TQ, :] = np.asarray(res.results[c]["out"])
    return out


# version 34 (final)
# speedup vs baseline: 1.0002x; 1.0002x over previous
import math
from contextlib import ExitStack

import numpy as np
import ml_dtypes
import concourse.bass as bass
import concourse.mybir as mybir
from concourse.bass_utils import run_bass_kernel_spmd

F32 = mybir.dt.float32
BF16 = mybir.dt.bfloat16
FP8 = mybir.dt.float8e4
I32 = mybir.dt.int32
AF = mybir.ActivationFunctionType
ALU = mybir.AluOpType
AXL = mybir.AxisListType

B, S, D, H, HD = 4, 4096, 1024, 16, 64
TQ = S // 2
NT = S // 128
NTQ = TQ // 128
EPS = 1e-5
QB = 512
NQB = TQ // QB
NKT = S // 128
VGH = 4
VSL = HD + 1
VROW = VGH * VSL

ENGINES = ("tensor", "vector", "scalar", "gpsimd", "sync")


class _Res:
    __slots__ = ("w", "rs", "name")

    def __init__(self, name=""):
        self.w = None
        self.rs = []
        self.name = name


class _Op:
    __slots__ = ("eng", "fn", "deps", "is_dma", "need_inc", "sem_val", "idx",
                 "is_bar", "dma_slot", "dma_n")

    def __init__(self, eng, fn, is_dma, idx):
        self.eng = eng
        self.fn = fn
        self.deps = []
        self.is_dma = is_dma
        self.need_inc = False
        self.sem_val = None
        self.idx = idx
        self.is_bar = False
        self.dma_slot = 0
        self.dma_n = -1


NDMASEM = 8


class Sched:

    def __init__(self):
        self.ops = {e: [] for e in ENGINES}
        self.n = 0

    def op(self, eng, fn, reads=(), writes=(), is_dma=False):
        o = _Op(eng, fn, is_dma, self.n)
        self.n += 1
        for r in reads:
            if r.w is not None:
                o.deps.append(r.w)
            r.rs.append(o)
        for r in writes:
            if r.w is not None:
                o.deps.append(r.w)
            o.deps.extend(r.rs)
            r.rs = []
            r.w = o
        self.ops[eng].append(o)
        return o

    def barrier(self):
        rb = {e: _Res(f"bar{self.n}_{e}") for e in ENGINES}
        for e in ENGINES:
            o = self.op(e, lambda eng: eng.nop(nofuse=True), writes=[rb[e]])
            o.is_bar = True
        allr = list(rb.values())
        for e in ENGINES:
            o = self.op(e, lambda eng: eng.nop(nofuse=True), reads=allr)
            o.is_bar = True

    def emit(self, nc, block):
        for e in ENGINES:
            for o in self.ops[e]:
                for d in o.deps:
                    if d.eng != o.eng or d.is_dma != o.is_dma:
                        d.need_inc = True
        for e in ENGINES:
            cnt_c = 0
            n_dma = 0
            cnt_b = 0
            for o in self.ops[e]:
                if o.is_dma:
                    o.need_inc = True
                    o.dma_slot = n_dma % NDMASEM
                    o.sem_val = 16 * (n_dma // NDMASEM + 1)
                    o.dma_n = n_dma
                    n_dma += 1
                elif o.is_bar and o.need_inc:
                    cnt_b += 1
                    o.sem_val = cnt_b
                elif o.need_inc:
                    cnt_c += 1
                    o.sem_val = cnt_c
        csem = {e: nc.semaphore(f"c_{e}").__enter__() for e in ENGINES}
        bsem = {e: nc.semaphore(f"b_{e}").__enter__() for e in ENGINES}
        dsem = {}
        for e in ENGINES:
            if any(o.is_dma for o in self.ops[e]):
                dsem[e] = [nc.semaphore(f"d_{e}{i}").__enter__()
                           for i in range(NDMASEM)]
        sched = self

        local_of = {}
        for e in ENGINES:
            for li, o in enumerate(self.ops[e]):
                local_of[id(o)] = li

        def runner(ename):
            def body(eng):
                waited = {}
                drained = -1
                prev_local = -1
                n_dma_seen = 0
                for o in sched.ops[ename]:
                    if o.is_bar and ename in dsem:
                        for j in range(NDMASEM):
                            cnt = (n_dma_seen - j + NDMASEM - 1) // NDMASEM
                            if cnt <= 0:
                                continue
                            v = 16 * cnt
                            key = (ename, "d", j)
                            if waited.get(key, -1) >= v:
                                continue
                            waited[key] = v
                            eng.wait_ge(dsem[ename][j], v)
                    need_drain = False
                    for d in o.deps:
                        if (d.eng == o.eng and d.is_dma and o.is_dma):
                            continue
                        if d.eng == o.eng and not d.is_dma and not o.is_dma:
                            if (ename in ("vector", "scalar", "gpsimd")
                                    and local_of[id(d)] > drained):
                                need_drain = True
                            continue
                        if d.is_dma:
                            for j in range(NDMASEM):
                                cnt = (d.dma_n + 1 - j + NDMASEM - 1) // NDMASEM
                                if cnt <= 0:
                                    continue
                                v = 16 * cnt
                                key = (d.eng, "d", j)
                                if waited.get(key, -1) >= v:
                                    continue
                                waited[key] = v
                                eng.wait_ge(dsem[d.eng][j], v)
                            continue
                        if d.is_bar:
                            sem = bsem[d.eng]
                            key = (d.eng, "b")
                        else:
                            sem = csem[d.eng]
                            key = (d.eng, "c")
                        v = d.sem_val
                        assert v is not None
                        if waited.get(key, -1) >= v:
                            continue
                        waited[key] = v
                        eng.wait_ge(sem, v)
                    if need_drain:
                        eng.drain()
                        drained = prev_local
                    if o.is_dma and o.sem_val > 16:
                        key = (ename, "d", o.dma_slot)
                        v = o.sem_val - 16
                        if waited.get(key, -1) < v:
                            waited[key] = v
                            eng.wait_ge(dsem[ename][o.dma_slot], v)
                    inst = o.fn(eng)
                    if o.is_dma:
                        n_dma_seen += 1
                    else:
                        prev_local = local_of[id(o)]
                    if o.need_inc:
                        assert inst is not None, (ename, o.idx)
                        if o.is_dma:
                            inst.then_inc(dsem[ename][o.dma_slot], 16)
                        elif o.is_bar:
                            inst.then_inc(bsem[ename], 1)
                        else:
                            inst.then_inc(csem[ename], 1)
            return body

        block.tensor(runner("tensor"))
        block.vector(runner("vector"))
        block.scalar(runner("scalar"))
        block.gpsimd(runner("gpsimd"))
        block.sync(runner("sync"))


def build_program(taps=(), phases=3):
    nc = bass.Bass("TRN2", target_bir_lowering=False)
    taps = set(taps)
    R = _Res
    sch = Sched()

    dx = nc.dram_tensor("x", [S, D], F32, kind="ExternalInput").ap()
    dwq = nc.dram_tensor("wqt", [D, D], F32, kind="ExternalInput").ap()
    dwk = nc.dram_tensor("wkt", [D, D], F32, kind="ExternalInput").ap()
    dwv = nc.dram_tensor("wvt", [D, D], F32, kind="ExternalInput").ap()
    dwo = nc.dram_tensor("wot", [D, D], F32, kind="ExternalInput").ap()
    dkern = nc.dram_tensor("kern", [D, D], BF16, kind="ExternalInput").ap()
    dbq = nc.dram_tensor("bqp", [128, 8], F32, kind="ExternalInput").ap()
    dbk = nc.dram_tensor("bkp", [128, 8], F32, kind="ExternalInput").ap()
    dbv = nc.dram_tensor("bvr", [1, D], F32, kind="ExternalInput").ap()
    dbo = nc.dram_tensor("bor", [1, D], F32, kind="ExternalInput").ap()
    dfb = nc.dram_tensor("fbias", [128, D], BF16, kind="ExternalInput").ap()
    dcos = nc.dram_tensor("cos_t", [S, D // 2], F32, kind="ExternalInput").ap()
    dsin = nc.dram_tensor("sin_t", [S, D // 2], F32, kind="ExternalInput").ap()
    did8 = nc.dram_tensor("id8", [128, 128], FP8, kind="ExternalInput").ap()
    didb = nc.dram_tensor("idb", [128, 128], BF16, kind="ExternalInput").ap()
    dsin_s = nc.dram_tensor("s_in", [128, 1], F32, kind="ExternalInput").ap()
    dsk_s = nc.dram_tensor("s_k", [128, 1], F32, kind="ExternalInput").ap()

    dout = nc.dram_tensor("out", [TQ, D], F32, kind="ExternalOutput").ap()

    dx1 = nc.dram_tensor("x1_sc", [TQ, D], F32).ap()
    drxh = nc.dram_tensor("rxh_sc", [8, 128, S], BF16).ap()
    drxl = nc.dram_tensor("rxl_sc", [8, 128, S], BF16).ap()
    dwsp = nc.dram_tensor("wsp_sc", [3, 2, 8, 128, D], BF16).ap()
    dwosp = nc.dram_tensor("wosp_sc", [2, 8, 128, D], BF16).ap()
    da = nc.dram_tensor("a_sc", [2, D, TQ], BF16).ap()

    tap_aps = {}

    def add_tap(name, shape, dt):
        if name in taps:
            tap_aps[name] = nc.dram_tensor("tap_" + name, shape, dt,
                                           kind="ExternalOutput").ap()

    add_tap("x1", [TQ, D], F32)
    add_tap("rxh", [8, 128, S], BF16)
    add_tap("rxl", [8, 128, S], BF16)
    add_tap("a", [2, D, TQ], BF16)

    with ExitStack() as top:
        ent = top.enter_context

        kern8 = ent(nc.sbuf_tensor("kern8", [128, 8 * D], FP8))
        fb_sb = ent(nc.sbuf_tensor("fb_sb", [128, D], BF16))
        bq_sb = ent(nc.sbuf_tensor("bq_sb", [128, 8], F32))
        bk_sb = ent(nc.sbuf_tensor("bk_sb", [128, 8], F32))
        bvrows = ent(nc.sbuf_tensor("bvrows", [2, D], BF16))
        borows = ent(nc.sbuf_tensor("borows", [2, D], BF16))
        ones2 = ent(nc.sbuf_tensor("ones2", [2, 128], BF16))
        id8_sb = ent(nc.sbuf_tensor("id8_sb", [128, 128], FP8))
        idb_sb = ent(nc.sbuf_tensor("idb_sb", [128, 128], BF16))
        sinv_sb = ent(nc.sbuf_tensor("sinv_sb", [128, 1], F32))
        stot_sb = ent(nc.sbuf_tensor("stot_sb", [128, 1], F32))
        scs = ent(nc.sbuf_tensor("scs", [128, 16], F32))
        ln_dump = ent(nc.sbuf_tensor("ln_dump", [128, D], F32))

        r_kern8, r_fb, r_bq, r_bk = R("kern8"), R("fb"), R("bq"), R("bk")
        r_bvrows, r_borows, r_ones2 = R("bvr"), R("bor"), R("ones2")
        r_id8, r_idb, r_sinv, r_stot = R("id8"), R("idb"), R("sinv"), R("stot")
        r_scs, r_dump = R("scs"), R("dump")
        r_dx1, r_drx, r_dwsp, r_dwosp, r_da, r_out = (
            R("dx1"), R("drx"), R("dwsp"), R("dwosp"), R("da"), R("out"))

        with ExitStack() as pre:
            pe = pre.enter_context
            kstage = pe(nc.sbuf_tensor("kstage", [128, 8 * D], BF16))
            brs = pe(nc.sbuf_tensor("brs", [1, D], F32))
            brl = pe(nc.sbuf_tensor("brl", [1, D], BF16))
            r_kstage, r_brs, r_brl = R(), R(), R()

            for dst, src, rr in ((id8_sb, did8, r_id8), (idb_sb, didb, r_idb),
                                 (fb_sb, dfb, r_fb), (bq_sb, dbq, r_bq),
                                 (bk_sb, dbk, r_bk)):
                sch.op("sync", (lambda dst, src: lambda e: e.dma_start(
                    out=dst[:, :], in_=src))(dst, src), writes=[rr], is_dma=True)
            sch.op("sync", lambda e: e.dma_start(out=scs[:, 0:1], in_=dsin_s),
                   writes=[r_scs], is_dma=True)
            sch.op("sync", lambda e: e.dma_start(out=scs[:, 1:2], in_=dsk_s),
                   writes=[r_scs], is_dma=True)
            sch.op("sync", lambda e: e.dma_start(
                out=kstage[:, :], in_=dkern.rearrange("(c p) d -> p c d", p=128)),
                writes=[r_kstage], is_dma=True)
            sch.op("vector", lambda e: e.memset(ones2[:, :], 1.0), writes=[r_ones2])

            def recip_ops(src_col, dst_col):
                sch.op("scalar", lambda e: e.activation(scs[:, 4:5], src_col, AF.Ln),
                       reads=[r_scs], writes=[r_scs])
                sch.op("scalar", lambda e: e.activation(scs[:, 5:6], scs[:, 4:5],
                                                        AF.Exp, scale=-1.0),
                       reads=[r_scs], writes=[r_scs])
                for _ in range(2):
                    sch.op("vector", lambda e: e.tensor_mul(scs[:, 4:5], src_col,
                                                            scs[:, 5:6]),
                           reads=[r_scs], writes=[r_scs])
                    sch.op("vector", lambda e: e.tensor_scalar(
                        scs[:, 4:5], scs[:, 4:5], -1.0, 2.0, ALU.mult, ALU.add),
                        reads=[r_scs], writes=[r_scs])
                    sch.op("vector", lambda e: e.tensor_mul(scs[:, 5:6], scs[:, 5:6],
                                                            scs[:, 4:5]),
                           reads=[r_scs], writes=[r_scs])
                sch.op("vector", lambda e: e.tensor_copy(dst_col, scs[:, 5:6]),
                       reads=[r_scs], writes=[r_scs])

            recip_ops(scs[:, 0:1], scs[:, 6:7])
            recip_ops(scs[:, 1:2], scs[:, 7:8])
            sch.op("vector", lambda e: e.tensor_copy(sinv_sb[:, :], scs[:, 6:7]),
                   reads=[r_scs], writes=[r_sinv])
            sch.op("vector", lambda e: e.tensor_mul(stot_sb[:, :], scs[:, 0:1],
                                                    scs[:, 1:2]),
                   reads=[r_scs], writes=[r_stot])
            sch.op("vector", lambda e: e.tensor_scalar(
                kern8[:, :], kstage[:, :], scs[:, 7:8], None, ALU.mult),
                reads=[r_kstage, r_scs], writes=[r_kern8])
            sch.op("vector", lambda e: e.tensor_copy(
                kern8[:, :].bitcast(I32), kern8[:, :].bitcast(I32)),
                reads=[r_kern8], writes=[r_kern8])

            for src, dst, rr in ((dbv, bvrows, r_bvrows), (dbo, borows, r_borows)):
                def _bias(src=src, dst=dst, rr=rr):
                    sch.op("sync", lambda e: e.dma_start(out=brs[0:1, :], in_=src),
                           writes=[r_brs], is_dma=True)
                    sch.op("vector", lambda e: e.tensor_copy(dst[0:1, :], brs[0:1, :]),
                           reads=[r_brs], writes=[rr])
                    sch.op("vector", lambda e: e.scalar_tensor_tensor(
                        brl[0:1, :], brs[0:1, :], 1.0, dst[0:1, :],
                        ALU.mult, ALU.subtract),
                        reads=[r_brs, rr], writes=[r_brl])
                    sch.op("gpsimd", lambda e: e.dma_start(out=dst[1:2, :],
                                                           in_=brl[0:1, :]),
                           reads=[r_brl], writes=[rr], is_dma=True)
                _bias()
            sch.barrier()

        with ExitStack() as pha:
            pa = pha.enter_context
            NB = 4
            xt = [pa(nc.sbuf_tensor(f"xt{i}", [128, D], F32)) for i in range(NB)]
            xb = [pa(nc.sbuf_tensor(f"xb{i}", [128, D], BF16)) for i in range(NB)]
            xqT = [pa(nc.sbuf_tensor(f"xqT{i}", [128, D], FP8)) for i in range(NB)]
            ybf = [pa(nc.sbuf_tensor(f"ybf{i}", [128, D], BF16)) for i in range(NB)]
            x1t = [pa(nc.sbuf_tensor(f"x1t{i}", [128, D], F32)) for i in range(NB)]
            rxf = [pa(nc.sbuf_tensor(f"rxf{i}", [128, D], F32)) for i in range(NB)]
            rxh_t = [pa(nc.sbuf_tensor(f"rxht{i}", [128, D], BF16)) for i in range(NB)]
            rxl_t = [pa(nc.sbuf_tensor(f"rxlt{i}", [128, D], BF16)) for i in range(NB)]
            rxhT = [pa(nc.sbuf_tensor(f"rxhT{i}", [128, D], BF16)) for i in range(NB)]
            rxlT = [pa(nc.sbuf_tensor(f"rxlT{i}", [128, D], BF16)) for i in range(NB)]
            cost = [pa(nc.sbuf_tensor(f"cost{i}", [128, D // 2], F32)) for i in range(NB)]
            sint = [pa(nc.sbuf_tensor(f"sint{i}", [128, D // 2], F32)) for i in range(NB)]
            rp1 = [pa(nc.sbuf_tensor(f"rp1_{i}", [128, D // 2], F32)) for i in range(NB)]
            rp2 = [pa(nc.sbuf_tensor(f"rp2_{i}", [128, D // 2], F32)) for i in range(NB)]
            stt = [pa(nc.sbuf_tensor(f"stt{i}", [128, 12], F32)) for i in range(NB)]
            wst = [pa(nc.sbuf_tensor(f"wst{i}", [128, D], F32)) for i in range(NB)]
            wsh = [pa(nc.sbuf_tensor(f"wsh{i}", [128, D], BF16)) for i in range(NB)]
            wsl = [pa(nc.sbuf_tensor(f"wsl{i}", [128, D], BF16)) for i in range(NB)]
            psY = [pa(nc.psum_tensor(f"psY{i}", [128, D], F32))
                   for i in range(2)]
            psT8 = [pa(nc.psum_tensor("psT8_0", [128, D], BF16))]
            psTh = [pa(nc.psum_tensor("psTh0", [128, D], BF16))]
            psTl = [pa(nc.psum_tensor("psTl0", [128, D], BF16))]

            r_xt = [R() for _ in range(NB)]
            r_xb = [R() for _ in range(NB)]
            r_xqT = [R() for _ in range(NB)]
            r_ybf = [R() for _ in range(NB)]
            r_x1t = [R() for _ in range(NB)]
            r_rxf = [R() for _ in range(NB)]
            r_rxh = [R() for _ in range(NB)]
            r_rxl = [R() for _ in range(NB)]
            r_rxhT = [R() for _ in range(NB)]
            r_rxlT = [R() for _ in range(NB)]
            r_cos = [R() for _ in range(NB)]
            r_sin = [R() for _ in range(NB)]
            r_rp1 = [R() for _ in range(NB)]
            r_rp2 = [R() for _ in range(NB)]
            r_stt = [R() for _ in range(NB)]
            r_wst = [R() for _ in range(NB)]
            r_wsh = [R() for _ in range(NB)]
            r_wsl = [R() for _ in range(NB)]
            r_psY = [R(), R()]
            r_psT8 = [R()]
            r_psTh = [R()]
            r_psTl = [R()]

            def layer_norm(src_ap, r_src_list, st, r_st, dst_ap, r_dst_list):
                rd = list(r_src_list)
                sch.op("vector", lambda e: e.reduce_sum(st[:, 0:1], src_ap, axis=AXL.X),
                       reads=rd, writes=[r_st])
                sch.op("vector", lambda e: e.scalar_tensor_tensor(
                    ln_dump[:, :], src_ap, 1.0, src_ap, ALU.mult, ALU.mult,
                    accum_out=st[:, 1:2]),
                    reads=rd, writes=[r_dump, r_st])
                sch.op("vector", lambda e: e.tensor_scalar(
                    st[:, 2:3], st[:, 0:1], -1.0 / D, None, ALU.mult),
                    reads=[r_st], writes=[r_st])
                sch.op("vector", lambda e: e.scalar_tensor_tensor(
                    st[:, 3:4], st[:, 2:3], 1.0, st[:, 2:3], ALU.mult, ALU.mult),
                    reads=[r_st], writes=[r_st])
                sch.op("vector", lambda e: e.scalar_tensor_tensor(
                    st[:, 4:5], st[:, 3:4], -D, st[:, 1:2], ALU.mult, ALU.add),
                    reads=[r_st], writes=[r_st])
                sch.op("vector", lambda e: e.tensor_scalar(
                    st[:, 4:5], st[:, 4:5], 1.0 / D, None, ALU.mult),
                    reads=[r_st], writes=[r_st])
                sch.op("vector", lambda e: e.tensor_scalar(
                    st[:, 9:10], st[:, 4:5], EPS, None, ALU.add),
                    reads=[r_st], writes=[r_st])
                sch.op("scalar", lambda e: e.activation(
                    st[:, 5:6], st[:, 9:10], AF.Ln),
                    reads=[r_st], writes=[r_st])
                sch.op("scalar", lambda e: e.activation(
                    st[:, 6:7], st[:, 5:6], AF.Exp, scale=-0.5),
                    reads=[r_st], writes=[r_st])
                sch.op("vector", lambda e: e.scalar_tensor_tensor(
                    st[:, 7:8], st[:, 6:7], 1.0, st[:, 6:7], ALU.mult, ALU.mult),
                    reads=[r_st], writes=[r_st])
                sch.op("vector", lambda e: e.scalar_tensor_tensor(
                    st[:, 8:9], st[:, 9:10], 1.0, st[:, 7:8], ALU.mult, ALU.mult),
                    reads=[r_st], writes=[r_st])
                sch.op("vector", lambda e: e.tensor_scalar(
                    st[:, 8:9], st[:, 8:9], -0.5, 1.5, ALU.mult, ALU.add),
                    reads=[r_st], writes=[r_st])
                sch.op("vector", lambda e: e.scalar_tensor_tensor(
                    st[:, 6:7], st[:, 8:9], 1.0, st[:, 6:7], ALU.mult, ALU.mult),
                    reads=[r_st], writes=[r_st])
                sch.op("vector", lambda e: e.tensor_scalar(
                    dst_ap, src_ap, st[:, 2:3], st[:, 6:7], ALU.add, ALU.mult),
                    reads=rd + [r_st], writes=list(r_dst_list))

            wsrc_list = [(dwq, 0), (dwk, 1), (dwv, 2), (dwo, 3)]
            HF = D // 2

            def stage0(t):
                i = t % NB
                sch.op("sync", lambda e, t=t, i=i: e.dma_start(
                    out=xt[i][:, :], in_=dx[t * 128:(t + 1) * 128, :]),
                    writes=[r_xt[i]], is_dma=True)
                sch.op("sync", lambda e, t=t, i=i: e.dma_start(
                    out=cost[i][:, :], in_=dcos[t * 128:(t + 1) * 128, :]),
                    writes=[r_cos[i]], is_dma=True)
                sch.op("sync", lambda e, t=t, i=i: e.dma_start(
                    out=sint[i][:, :], in_=dsin[t * 128:(t + 1) * 128, :]),
                    writes=[r_sin[i]], is_dma=True)
                wsrc = wsrc_list[t // 8][0]
                chunk = t % 8
                sch.op("sync", lambda e, wsrc=wsrc, chunk=chunk, i=i: e.dma_start(
                    out=wst[i][:, :], in_=wsrc[chunk * 128:(chunk + 1) * 128, :]),
                    writes=[r_wst[i]], is_dma=True)

            def stage1(t):
                i = t % NB
                tb = 0
                layer_norm(xt[i][:, :], [r_xt[i]], stt[i], r_stt[i],
                           xb[i][:, :], [r_xb[i]])
                for c in range(8):
                    sch.op("tensor", lambda e, i=i, c=c, tb=tb: e.transpose(
                        psT8[tb][:, c * 128:(c + 1) * 128],
                        xb[i][:, c * 128:(c + 1) * 128], idb_sb[:, :]),
                        reads=[r_xb[i], r_idb], writes=[r_psT8[tb]])
                sch.op("vector", lambda e, i=i, tb=tb: e.tensor_scalar(
                    xqT[i][:, :], psT8[tb][:, 0:D], sinv_sb[:, 0:1], None, ALU.mult),
                    reads=[r_psT8[tb], r_sinv], writes=[r_xqT[i]])
                sch.op("vector", lambda e, i=i: e.tensor_copy(
                    xqT[i][:, :].bitcast(I32), xqT[i][:, :].bitcast(I32)),
                    reads=[r_xqT[i]], writes=[r_xqT[i]])

            def stage2(t):
                i = t % NB
                yb = t % 2
                for half in range(2):
                    for c in range(8):
                        sch.op("tensor", lambda e, i=i, c=c, half=half, yb=yb: e.matmul(
                            psY[yb][:, half * 512:(half + 1) * 512],
                            xqT[i][:, c * 128:(c + 1) * 128],
                            kern8[:, c * D + half * 512:c * D + half * 512 + 512],
                            start=(c == 0), stop=(c == 7)),
                            reads=[r_xqT[i], r_kern8], writes=[r_psY[yb]])
                for half in range(2):
                    sch.op("vector", lambda e, i=i, half=half, yb=yb:
                           e.scalar_tensor_tensor(
                               ybf[i][:, half * 512:(half + 1) * 512],
                               psY[yb][:, half * 512:(half + 1) * 512],
                               stot_sb[:, 0:1],
                               fb_sb[:, half * 512:(half + 1) * 512],
                               ALU.mult, ALU.add),
                           reads=[r_psY[yb], r_stot, r_fb], writes=[r_ybf[i]])
                sch.op("vector", lambda e, i=i: e.tensor_copy(
                    ybf[i][:, :].bitcast(I32), ybf[i][:, :].bitcast(I32)),
                    reads=[r_ybf[i]], writes=[r_ybf[i]])
                sch.op("vector", lambda e, i=i: e.scalar_tensor_tensor(
                    x1t[i][:, :], ybf[i][:, :], 0.5, xt[i][:, :], ALU.mult, ALU.add),
                    reads=[r_ybf[i], r_xt[i]], writes=[r_x1t[i]])
                if t < NTQ:
                    sch.op("sync", lambda e, t=t, i=i: e.dma_start(
                        out=dx1[t * 128:(t + 1) * 128, :], in_=x1t[i][:, :]),
                        reads=[r_x1t[i]], writes=[r_dx1], is_dma=True)
                layer_norm(x1t[i][:, :], [r_x1t[i]], stt[i], r_stt[i],
                           rxf[i][:, :], [r_rxf[i]])

            def stage3(t):
                i = t % NB
                tb = 0
                sch.op("vector", lambda e, i=i: e.tensor_mul(
                    rp1[i][:, :], rxf[i][:, 0:HF], cost[i][:, :]),
                    reads=[r_rxf[i], r_cos[i]], writes=[r_rp1[i]])
                sch.op("gpsimd", lambda e, i=i: e.tensor_mul(
                    rp2[i][:, :], rxf[i][:, HF:D], sint[i][:, :]),
                    reads=[r_rxf[i], r_sin[i]], writes=[r_rp2[i]])
                sch.op("vector", lambda e, i=i: e.tensor_sub(
                    rp1[i][:, :], rp1[i][:, :], rp2[i][:, :]),
                    reads=[r_rp2[i]], writes=[r_rp1[i]])
                sch.op("gpsimd", lambda e, i=i: e.tensor_mul(
                    rp2[i][:, :], rxf[i][:, 0:HF], sint[i][:, :]),
                    reads=[r_rxf[i], r_sin[i]], writes=[r_rp2[i]])
                sch.op("vector", lambda e, i=i: e.scalar_tensor_tensor(
                    rxf[i][:, HF:D], rxf[i][:, HF:D], 1.0, cost[i][:, :],
                    ALU.mult, ALU.mult),
                    reads=[r_cos[i]], writes=[r_rxf[i]])
                sch.op("vector", lambda e, i=i: e.tensor_add(
                    rxf[i][:, HF:D], rxf[i][:, HF:D], rp2[i][:, :]),
                    reads=[r_rp2[i]], writes=[r_rxf[i]])
                sch.op("vector", lambda e, i=i: e.tensor_copy(
                    rxf[i][:, 0:HF], rp1[i][:, :]),
                    reads=[r_rp1[i]], writes=[r_rxf[i]])
                sch.op("vector", lambda e, i=i: e.tensor_copy(
                    rxh_t[i][:, :], rxf[i][:, :]),
                    reads=[r_rxf[i]], writes=[r_rxh[i]])
                sch.op("vector", lambda e, i=i: e.tensor_sub(
                    rxl_t[i][:, :], rxf[i][:, :], rxh_t[i][:, :]),
                    reads=[r_rxf[i], r_rxh[i]], writes=[r_rxl[i]])
                for c in range(8):
                    sch.op("tensor", lambda e, i=i, c=c, tb=tb: e.transpose(
                        psTh[tb][:, c * 128:(c + 1) * 128],
                        rxh_t[i][:, c * 128:(c + 1) * 128], idb_sb[:, :]),
                        reads=[r_rxh[i], r_idb], writes=[r_psTh[tb]])
                    sch.op("tensor", lambda e, i=i, c=c, tb=tb: e.transpose(
                        psTl[tb][:, c * 128:(c + 1) * 128],
                        rxl_t[i][:, c * 128:(c + 1) * 128], idb_sb[:, :]),
                        reads=[r_rxl[i], r_idb], writes=[r_psTl[tb]])
                sch.op("scalar", lambda e, i=i, tb=tb: e.copy(
                    rxhT[i][:, :], psTh[tb][:, :]),
                    reads=[r_psTh[tb]], writes=[r_rxhT[i]])
                sch.op("vector", lambda e, i=i, tb=tb: e.tensor_copy(
                    rxlT[i][:, :], psTl[tb][:, :]),
                    reads=[r_psTl[tb]], writes=[r_rxlT[i]])
                sch.op("sync", lambda e, t=t, i=i: e.dma_start(
                    out=drxh[:, :, t * 128:(t + 1) * 128].rearrange("c p q -> p c q"),
                    in_=rxhT[i][:, :]),
                    reads=[r_rxhT[i]], writes=[r_drx], is_dma=True)
                sch.op("gpsimd", lambda e, t=t, i=i: e.dma_start(
                    out=drxl[:, :, t * 128:(t + 1) * 128].rearrange("c p q -> p c q"),
                    in_=rxlT[i][:, :]),
                    reads=[r_rxlT[i]], writes=[r_drx], is_dma=True)
                wi = wsrc_list[t // 8][1]
                chunk = t % 8
                sch.op("vector", lambda e, i=i: e.tensor_copy(
                    wsh[i][:, :], wst[i][:, :]),
                    reads=[r_wst[i]], writes=[r_wsh[i]])
                sch.op("vector", lambda e, i=i: e.tensor_sub(
                    wsl[i][:, :], wst[i][:, :], wsh[i][:, :]),
                    reads=[r_wst[i], r_wsh[i]], writes=[r_wsl[i]])
                if wi < 3:
                    sch.op("gpsimd", lambda e, wi=wi, chunk=chunk, i=i: e.dma_start(
                        out=dwsp[wi, 0, chunk, :, :], in_=wsh[i][:, :]),
                        reads=[r_wsh[i]], writes=[r_dwsp], is_dma=True)
                    sch.op("gpsimd", lambda e, wi=wi, chunk=chunk, i=i: e.dma_start(
                        out=dwsp[wi, 1, chunk, :, :], in_=wsl[i][:, :]),
                        reads=[r_wsl[i]], writes=[r_dwsp], is_dma=True)
                else:
                    sch.op("gpsimd", lambda e, chunk=chunk, i=i: e.dma_start(
                        out=dwosp[0, chunk, :, :], in_=wsh[i][:, :]),
                        reads=[r_wsh[i]], writes=[r_dwosp], is_dma=True)
                    sch.op("gpsimd", lambda e, chunk=chunk, i=i: e.dma_start(
                        out=dwosp[1, chunk, :, :], in_=wsl[i][:, :]),
                        reads=[r_wsl[i]], writes=[r_dwosp], is_dma=True)

            for step in range(NT + 3):
                for s, fn in ((3, stage3), (2, stage2), (1, stage1), (0, stage0)):
                    tt = step - s
                    if 0 <= tt < NT:
                        fn(tt)

            if "x1" in tap_aps:
                sch.op("sync", lambda e: e.dma_start(out=tap_aps["x1"], in_=dx1),
                       reads=[r_dx1], writes=[r_out], is_dma=True)
            if "rxh" in tap_aps:
                sch.op("sync", lambda e: e.dma_start(out=tap_aps["rxh"], in_=drxh),
                       reads=[r_drx], writes=[r_out], is_dma=True)
            if "rxl" in tap_aps:
                sch.op("sync", lambda e: e.dma_start(out=tap_aps["rxl"], in_=drxl),
                       reads=[r_drx], writes=[r_out], is_dma=True)
            sch.barrier()

        if phases >= 2:
            with ExitStack() as phb:
                pb = phb.enter_context
                rxmh = pb(nc.sbuf_tensor("rxmh", [128, 8 * TQ], BF16))
                rxml = pb(nc.sbuf_tensor("rxml", [128, 8 * TQ], BF16))
                SBW = 256
                rxsh = [pb(nc.sbuf_tensor(f"rxsh{i}", [128, 8 * SBW], BF16))
                        for i in range(2)]
                rxsl = [pb(nc.sbuf_tensor(f"rxsl{i}", [128, 8 * SBW], BF16))
                        for i in range(2)]
                vgh = pb(nc.sbuf_tensor("vgh", [128, NKT * VROW], BF16))
                vgl = pb(nc.sbuf_tensor("vgl", [128, NKT * VROW], BF16))
                kst = [pb(nc.sbuf_tensor(f"kst{h}", [128, S], BF16)) for h in range(2)]
                kstg = pb(nc.sbuf_tensor("kstg", [128, S], BF16))
                qst = [pb(nc.sbuf_tensor(f"qst{h}", [128, TQ], BF16)) for h in range(2)]
                qlo = [pb(nc.sbuf_tensor(f"qlo{h}", [64, TQ], BF16))
                       for h in range(2)]
                qsg = kstg
                wph = pb(nc.sbuf_tensor("wph", [128, 8 * 128], BF16))
                wpl = pb(nc.sbuf_tensor("wpl", [128, 8 * 128], BF16))
                wqh = pb(nc.sbuf_tensor("wqh", [128, 8 * 128], BF16))
                wql = pb(nc.sbuf_tensor("wql", [128, 8 * 128], BF16))
                wvh_t = pb(nc.sbuf_tensor("wvh_t", [128, 8 * 256], BF16))
                wvl_t = pb(nc.sbuf_tensor("wvl_t", [128, 8 * 256], BF16))
                NPB = 4
                p32 = [pb(nc.sbuf_tensor(f"p32_{i}", [128, QB], F32))
                       for i in range(NPB)]
                phv = [p32[i][:, :].bitcast(BF16)[:, 1::2]
                       for i in range(NPB)]
                pstl = [pb(nc.sbuf_tensor(f"pstl{i}", [128, QB], BF16))
                        for i in range(NPB)]
                ascr = pb(nc.sbuf_tensor("ascr", [128, 4 * QB], F32))
                aun = ascr[0:64, 0:QB]
                ahi = ascr[0:64, QB:QB + QB // 2].bitcast(BF16)
                alo = ascr[0:64, QB + QB // 2:2 * QB].bitcast(BF16)
                rrows = ascr[0:2, 2 * QB:2 * QB + QB // 2].bitcast(BF16)
                gmax = ascr[0:1, 2 * QB + QB // 2:2 * QB + QB // 2 + 600]
                mxs = pb(nc.sbuf_tensor("mxs", [128, 48], F32))
                mxb = pb(nc.sbuf_tensor("mxb", [128, 4], BF16))
                psS = [pb(nc.psum_tensor(f"psS{i}", [128, QB], F32))
                        for i in range(4)]
                psKQ = psS
                psA = [pb(nc.psum_tensor(f"psA{i}", [128, QB], F32))
                       for i in range(2)]
                psRb = pb(nc.psum_tensor("psRb", [128, QB], F32))
                psRc = pb(nc.psum_tensor("psRc", [128, QB], F32))

                r_rxm = R("rxm")
                r_rxs = [R(), R()]
                r_vg = R("vg")
                r_kst = [R(), R()]
                r_kstg = R("kstg")
                r_qst = [R(), R()]
                r_qlo = [R(), R()]
                r_kasm = [R(), R()]
                r_qasm = [R(), R()]
                r_qsg = r_kstg
                r_wp = R("wp")
                r_wq = R("wq")
                r_wvh, r_wvl = R("wvh"), R("wvl")
                r_p32 = [R() for _ in range(NPB)]
                r_pstl = [R() for _ in range(NPB)]
                r_aun, r_rrows = R("aun"), R("rrows")
                r_ahi, r_alo, r_mxs, r_mxb = R("ahi"), R("alo"), R("mxs"), R("mxb")
                r_gmax = R("gmax")
                r_mxsg = R("mxsg")
                r_mxsa = [R() for _ in range(4)]
                r_g2, r_g3, r_g4, r_g5 = R(), R(), R(), R()
                r_psS = [R(f"psS{i}") for i in range(4)]
                r_psKQ = r_psS
                r_psA = [R("psA0"), R("psA1")]
                r_psRb = R("psRb")
                r_psRc = R("psRc")
                pending = [None]

                dln = ascr
                r_dln = R("dln")

                for c in range(8):
                    sch.op("sync", lambda e, c=c: e.dma_start(
                        out=rxmh[:, c * TQ:(c + 1) * TQ], in_=drxh[c, :, 0:TQ]),
                        reads=[r_drx], writes=[r_rxm], is_dma=True)
                    sch.op("sync", lambda e, c=c: e.dma_start(
                        out=rxml[:, c * TQ:(c + 1) * TQ], in_=drxl[c, :, 0:TQ]),
                        reads=[r_drx], writes=[r_rxm], is_dma=True)
                FLV = NKT * VROW
                sch.op("vector", lambda e: e.memset(
                    bass.AP(vgh, 64, [[FLV, 128], [VSL, NKT * VGH], [1, 1]]), 1.0),
                    writes=[r_vg])
                sch.op("vector", lambda e: e.memset(
                    bass.AP(vgl, 64, [[FLV, 128], [VSL, NKT * VGH], [1, 1]]), 0.0),
                    writes=[r_vg])

                def rx_slice(hilo, c, k0, width):
                    if k0 < TQ:
                        tns = rxmh if hilo == 0 else rxml
                        return (tns[:, c * TQ + k0: c * TQ + k0 + width], r_rxm)
                    blk = (k0 - TQ) // SBW
                    off = (k0 - TQ) % SBW
                    bi = blk % 2
                    tns = rxsh[bi] if hilo == 0 else rxsl[bi]
                    return (tns[:, c * SBW + off: c * SBW + off + width], r_rxs[bi])

                def load_block(blk):
                    bi = blk % 2
                    k0 = TQ + blk * SBW
                    sch.op("sync", lambda e, k0=k0, bi=bi: e.dma_start(
                        out=rxsh[bi][:, :],
                        in_=drxh[:, :, k0:k0 + SBW].rearrange("c p q -> p c q")),
                        reads=[r_drx], writes=[r_rxs[bi]], is_dma=True)
                    sch.op("sync", lambda e, k0=k0, bi=bi: e.dma_start(
                        out=rxsl[bi][:, :],
                        in_=drxl[:, :, k0:k0 + SBW].rearrange("c p q -> p c q")),
                        reads=[r_drx], writes=[r_rxs[bi]], is_dma=True)

                def gram_group(hs, src, r_src, g8, ntile, base_col):
                    bank = psRb if hs == 0 else psRc
                    r_bank = r_psRb if hs == 0 else r_psRc

                    def g_mms(e, g8=g8, src=src, ntile=ntile, bank=bank):
                        last = None
                        for t4 in range(4):
                            kt = g8 * 4 + t4
                            if kt >= ntile:
                                break
                            last = e.matmul(
                                bank[:, t4 * 128:(t4 + 1) * 128],
                                src[:, kt * 128:(kt + 1) * 128],
                                src[:, kt * 128:(kt + 1) * 128],
                                start=True, stop=True)
                        return last
                    sch.op("tensor", g_mms, reads=[r_src], writes=[r_bank])
                    nt4 = min(4, ntile - g8 * 4) * 128
                    sch.op("vector", lambda e, nt4=nt4, bank=bank,
                           col=base_col + g8: e.reduce_max(
                               mxs[:, col:col + 1], bank[:, 0:nt4],
                               axis=AXL.X,
                               apply_absolute_value=True),
                           reads=[r_bank], writes=[r_mxsg])

                def gram_final_both():
                    ngk = (NKT + 3) // 4
                    ngq = (NQB + 3) // 4
                    for k, (c0, n0) in enumerate(
                            ((16, ngk), (24, ngq), (32, ngk), (40, ngq))):
                        sch.op("vector", lambda e, k=k, c0=c0, n0=n0:
                               e.reduce_max(
                                   mxs[:, 1 + k:2 + k], mxs[:, c0:c0 + n0],
                                   axis=AXL.X),
                               reads=[r_mxsg], writes=[r_mxsa[k]])
                        sch.op("gpsimd", lambda e, k=k: e.dma_start(
                            out=gmax[0:1, k * 128:(k + 1) * 128],
                            in_=mxs[:, 1 + k:2 + k]),
                            reads=[r_mxsa[k]], writes=[r_gmax], is_dma=True)
                    sch.op("vector", lambda e: e.reduce_max(
                        gmax[0:1, 512:516],
                        gmax[0:1, 0:512].rearrange("p (f c) -> p f c", c=128),
                        axis=AXL.X),
                        reads=[r_gmax], writes=[r_g2])
                    sch.op("scalar", lambda e: e.activation(
                        gmax[0:1, 516:520], gmax[0:1, 512:516], AF.Ln),
                        reads=[r_g2], writes=[r_g3])
                    sch.op("vector", lambda e: e.tensor_add(
                        gmax[0:1, 520:522],
                        gmax[0:1, 516:520][:, 0::2],
                        gmax[0:1, 516:520][:, 1::2]),
                        reads=[r_g3], writes=[r_g4])
                    sch.op("scalar", lambda e: e.activation(
                        gmax[0:1, 522:524], gmax[0:1, 520:522],
                        AF.Exp, scale=0.5),
                        reads=[r_g4], writes=[r_g5])
                    sch.op("vector", lambda e: e.tensor_scalar(
                        mxb[0:1, 0:2], gmax[0:1, 522:524], -0.15,
                        None, ALU.mult),
                        reads=[r_g5], writes=[r_mxb])

                def shift_bcast(hs):
                    sch.op("tensor", lambda e, hs=hs: e.matmul(
                        psRc[:, 0:1], ones2[0:1, 0:128], mxb[0:1, hs:hs + 1],
                        start=True, stop=True),
                        reads=[r_mxb, r_ones2], writes=[r_psRc])
                    sch.op("scalar", lambda e, hs=hs: e.copy(
                        mxs[:, 9 + hs:10 + hs], psRc[:, 0:1]),
                        reads=[r_psRc], writes=[r_mxs])

                def attn_pair(pair):
                    LA = 4
                    total = NQB * NKT
                    tot2 = 2 * total

                    def emit_S(gi):
                        hd = gi // total
                        lk = gi % total
                        qb, kt = divmod(lk, NKT)
                        q0 = qb * QB
                        si = gi % NPB
                        bc = 9 + hd
                        sch.op("tensor", lambda e, kt=kt, q0=q0, si=si, hd=hd: (
                            e.matmul(psS[si][:, :],
                                     kst[hd][:, kt * 128:(kt + 1) * 128],
                                     qst[hd][:, q0:q0 + QB],
                                     start=True, stop=False),
                            e.matmul(psS[si][:, :],
                                     kst[hd][0:64, kt * 128:(kt + 1) * 128],
                                     qlo[hd][:, q0:q0 + QB],
                                     start=False, stop=True))[-1],
                            reads=[r_kst[hd], r_qst[hd], r_qlo[hd],
                                   r_kasm[hd], r_qasm[hd]],
                            writes=[r_psS[si]])
                        sch.op("scalar", lambda e, si=si, bc=bc: e.activation(
                            p32[si][:, :], psS[si][:, :], AF.Exp,
                            bias=mxs[:, bc:bc + 1], scale=0.125),
                            reads=[r_psS[si], r_mxs], writes=[r_p32[si]])
                        sch.op("vector", lambda e, si=si: e.tensor_sub(
                            pstl[si][:, :], p32[si][:, :], phv[si]),
                            reads=[r_p32[si]], writes=[r_pstl[si]])

                    def emit_AV(gj):
                        hd = gj // total
                        lk = gj % total
                        qb, kt = divmod(lk, NKT)
                        sj = gj % NPB
                        ab = (gj // NKT) % 2
                        vc = ((pair * 2 + hd) % VGH) * VSL
                        c0 = kt * VROW + vc
                        sch.op("tensor", lambda e, c0=c0, sj=sj, kt=kt, ab=ab: (
                            e.matmul(psA[ab][0:VSL, :], vgh[:, c0:c0 + VSL],
                                     phv[sj], start=(kt == 0), stop=False),
                            e.matmul(psA[ab][0:VSL, :], vgl[:, c0:c0 + VSL],
                                     phv[sj], start=False, stop=False),
                            e.matmul(psA[ab][0:VSL, :], vgh[:, c0:c0 + VSL],
                                     pstl[sj][:, :], start=False,
                                     stop=(kt == NKT - 1)))[-1],
                            reads=[r_vg, r_p32[sj], r_pstl[sj]],
                            writes=[r_psA[ab]])

                    def finish_evac(gqb):
                        ab = gqb % 2
                        sch.op("scalar", lambda e, ab=ab: e.copy(
                            aun[:, :], psA[ab][0:64, :]),
                            reads=[r_psA[ab]], writes=[r_aun])
                        sch.op("scalar", lambda e, ab=ab: e.copy(
                            dln[64:65, 0:QB], psA[ab][64:65, :]),
                            reads=[r_psA[ab]], writes=[r_dln])
                        sch.op("scalar", lambda e: e.activation(
                            dln[64:65, QB:2 * QB], dln[64:65, 0:QB], AF.Ln),
                            reads=[r_dln], writes=[r_dln])
                        sch.op("scalar", lambda e: e.activation(
                            dln[64:65, QB:2 * QB], dln[64:65, QB:2 * QB],
                            AF.Exp, scale=-1.0),
                            reads=[r_dln], writes=[r_dln])
                        sch.op("vector", lambda e: e.tensor_mul(
                            dln[64:65, 2 * QB:3 * QB], dln[64:65, 0:QB],
                            dln[64:65, QB:2 * QB]),
                            reads=[r_dln], writes=[r_dln])
                        sch.op("vector", lambda e: e.tensor_scalar(
                            dln[64:65, 2 * QB:3 * QB], dln[64:65, 2 * QB:3 * QB],
                            -1.0, 2.0, ALU.mult, ALU.add),
                            reads=[r_dln], writes=[r_dln])
                        sch.op("vector", lambda e: e.tensor_mul(
                            dln[64:65, QB:2 * QB], dln[64:65, QB:2 * QB],
                            dln[64:65, 2 * QB:3 * QB]),
                            reads=[r_dln], writes=[r_dln])
                        sch.op("vector", lambda e: e.tensor_copy(
                            dln[64:65, 3 * QB:3 * QB + QB // 2].bitcast(BF16),
                            dln[64:65, QB:2 * QB]),
                            reads=[r_dln], writes=[r_dln])
                        sch.op("vector", lambda e: e.tensor_sub(
                            dln[64:65, 3 * QB + QB // 2:4 * QB].bitcast(BF16),
                            dln[64:65, QB:2 * QB],
                            dln[64:65, 3 * QB:3 * QB + QB // 2].bitcast(BF16)),
                            reads=[r_dln], writes=[r_dln])
                        sch.op("gpsimd", lambda e: e.dma_start(
                            out=rrows[0:1, :],
                            in_=dln[64:65, 3 * QB:3 * QB + QB // 2].bitcast(BF16)),
                            reads=[r_dln], writes=[r_rrows], is_dma=True)
                        sch.op("gpsimd", lambda e: e.dma_start(
                            out=rrows[1:2, :],
                            in_=dln[64:65, 3 * QB + QB // 2:4 * QB].bitcast(BF16)),
                            reads=[r_dln], writes=[r_rrows], is_dma=True)

                    def finish_norm(gqb, bank=None, r_bank=None):
                        if bank is None:
                            bank, r_bank = psRb, r_psRb
                        h = pair * 2 + gqb // NQB
                        q0 = (gqb % NQB) * QB
                        sch.op("tensor", lambda e, bank=bank: e.matmul(
                            bank[0:64, :], ones2[0:2, 0:64], rrows[0:2, :],
                            start=True, stop=True),
                            reads=[r_rrows, r_ones2], writes=[r_bank])
                        sch.op("vector", lambda e, bank=bank: e.tensor_mul(
                            aun[:, :], aun[:, :], bank[0:64, :]),
                            reads=[r_bank], writes=[r_aun])
                        sch.op("scalar", lambda e: e.copy(ahi[:, :], aun[:, :]),
                               reads=[r_aun], writes=[r_ahi])
                        sch.op("vector", lambda e: e.tensor_sub(
                            alo[:, :], aun[:, :], ahi[:, :]),
                            reads=[r_aun, r_ahi], writes=[r_alo])
                        sch.op("gpsimd", lambda e, h=h, q0=q0: e.dma_start(
                            out=da[0, h * 64:(h + 1) * 64, q0:q0 + QB],
                            in_=ahi[:, :]),
                            reads=[r_ahi], writes=[r_da], is_dma=True)
                        sch.op("gpsimd", lambda e, h=h, q0=q0: e.dma_start(
                            out=da[1, h * 64:(h + 1) * 64, q0:q0 + QB],
                            in_=alo[:, :]),
                            reads=[r_alo], writes=[r_da], is_dma=True)

                    def warm_fill(nmm):
                        def mms(e, nmm=nmm):
                            last = None
                            for _ in range(nmm):
                                last = e.matmul(
                                    psRc[:, :], kst[0][:, 0:128],
                                    qst[0][:, 0:QB], start=True, stop=True)
                            return last
                        sch.op("tensor", mms,
                               reads=[r_kst[0], r_qst[0], r_kasm[0],
                                      r_qasm[0]],
                               writes=[r_psRc])

                    for gi in range(tot2 + LA):
                        if gi == 0:
                            shift_bcast(0)
                        if gi == total - 2:
                            shift_bcast(1)
                        if gi == 6 and pending[0] is not None:
                            pending[0]()
                            pending[0] = None
                        if gi == 4:
                            warm_fill(3)
                        elif gi == 5:
                            warm_fill(2)
                        if gi >= LA:
                            gj = gi - LA
                            kt = gj % NKT
                            gqb = gj // NKT
                            emit_AV(gj)
                            if kt == NKT - 1:
                                finish_evac(gqb)
                            if kt == NKT // 2 and gqb > 0:
                                finish_norm(gqb - 1)
                        if gi < tot2:
                            emit_S(gi)
                    gq_last = 2 * NQB - 1
                    pending[0] = (lambda fn=finish_norm: fn(
                        gq_last, psA[gq_last % 2], r_psA[gq_last % 2]))

                NOB = TQ // SBW

                def load_vw(g):
                    gf0 = g * VGH * HD
                    sch.op("gpsimd", lambda e, gf0=gf0: e.dma_start(
                        out=wvh_t[:, :],
                        in_=dwsp[2, 0, :, :, gf0:gf0 + 256]
                        .rearrange("c p q -> p c q")),
                        reads=[r_dwsp], writes=[r_wvh], is_dma=True)
                    sch.op("gpsimd", lambda e, gf0=gf0: e.dma_start(
                        out=wvl_t[:, :],
                        in_=dwsp[2, 1, :, :, gf0:gf0 + 256]
                        .rearrange("c p q -> p c q")),
                        reads=[r_dwsp], writes=[r_wvl], is_dma=True)

                def load_qw(pair):
                    pf0 = pair * 128
                    sch.op("gpsimd", lambda e, pf0=pf0: e.dma_start(
                        out=wqh[:, :],
                        in_=dwsp[0, 0, :, :, pf0:pf0 + 128]
                        .rearrange("c p q -> p c q")),
                        reads=[r_dwsp], writes=[r_wq], is_dma=True)
                    sch.op("gpsimd", lambda e, pf0=pf0: e.dma_start(
                        out=wql[:, :],
                        in_=dwsp[0, 1, :, :, pf0:pf0 + 128]
                        .rearrange("c p q -> p c q")),
                        reads=[r_dwsp], writes=[r_wq], is_dma=True)

                def load_pw(kind, pair):
                    pf0 = pair * 128
                    sch.op("gpsimd", lambda e, pf0=pf0, kind=kind: e.dma_start(
                        out=wph[:, :],
                        in_=dwsp[kind, 0, :, :, pf0:pf0 + 128]
                        .rearrange("c p q -> p c q")),
                        reads=[r_dwsp], writes=[r_wp], is_dma=True)
                    sch.op("gpsimd", lambda e, pf0=pf0, kind=kind: e.dma_start(
                        out=wpl[:, :],
                        in_=dwsp[kind, 1, :, :, pf0:pf0 + 128]
                        .rearrange("c p q -> p c q")),
                        reads=[r_dwsp], writes=[r_wp], is_dma=True)

                def warm_boundary(nmm):
                    def mms(e, nmm=nmm):
                        last = None
                        for _ in range(nmm):
                            last = e.matmul(
                                psRc[:, :], rxmh[:, 0:128],
                                rxmh[:, 0:QB], start=True, stop=True)
                        return last
                    sch.op("tensor", mms, reads=[r_rxm], writes=[r_psRc])

                load_vw(0)
                load_pw(1, 0)
                load_qw(0)
                for g in range(H // VGH):
                    gf0 = g * VGH * HD
                    for kt in range(NKT):
                        k0 = kt * 128
                        if kt == 0 and g > 0:
                            warm_boundary(3)
                        if kt == 4:
                            load_block(0)
                        elif kt == 8:
                            load_block(1)
                        elif kt >= 18 and kt % 2 == 0 and (kt - 18) // 2 + 2 < NOB:
                            load_block((kt - 18) // 2 + 2)
                        si = kt % 2
                        rsrc = rx_slice(0, 0, k0, 128)[1]

                        def v_mms(e, k0=k0, si=si, gf0=gf0):
                            last = None
                            for c in range(8):
                                rxh_ap, _ = rx_slice(0, c, k0, 128)
                                rxl_ap, _ = rx_slice(1, c, k0, 128)
                                last = e.matmul(psKQ[si][:, 0:256], rxh_ap,
                                                wvh_t[:, c * 256:(c + 1) * 256],
                                                start=(c == 0), stop=False)
                                last = e.matmul(psKQ[si][:, 0:256], rxh_ap,
                                                wvl_t[:, c * 256:(c + 1) * 256],
                                                start=False, stop=False)
                                last = e.matmul(psKQ[si][:, 0:256], rxl_ap,
                                                wvh_t[:, c * 256:(c + 1) * 256],
                                                start=False, stop=False)
                            last = e.matmul(psKQ[si][:, 0:256], ones2[0:2, 0:128],
                                            bvrows[0:2, gf0:gf0 + 256],
                                            start=False, stop=True)
                            return last
                        sch.op("tensor", v_mms,
                               reads=[r_wvh, r_wvl, r_bvrows, r_ones2, rsrc],
                               writes=[r_psKQ[si]])
                        sch.op("scalar", lambda e, kt=kt, si=si: e.copy(
                            bass.AP(vgh, kt * VROW, [[FLV, 128], [VSL, VGH], [1, HD]]),
                            psKQ[si][:, 0:256]),
                            reads=[r_psKQ[si]], writes=[r_vg])
                        sch.op("vector", lambda e, kt=kt, si=si: e.scalar_tensor_tensor(
                            bass.AP(vgl, kt * VROW, [[FLV, 128], [VSL, VGH], [1, HD]]),
                            psKQ[si][:, 0:256], 1.0,
                            bass.AP(vgh, kt * VROW, [[FLV, 128], [VSL, VGH], [1, HD]]),
                            ALU.mult, ALU.subtract),
                            reads=[r_psKQ[si], r_vg], writes=[r_vg])

                    for pi in range(2):
                        pair = g * 2 + pi
                        pf0 = pair * 128
                        NCH = S // 256
                        for n in range(NCH):
                            k0 = n * 256
                            if n == 2 and pending[0] is not None:
                                pending[0]()
                                pending[0] = None
                            if n == 8:
                                load_block(0)
                                load_block(1)
                            elif 9 <= n <= NCH - 2:
                                load_block(n - 7)
                            si = n % 2
                            rsrc = rx_slice(0, 0, k0, 256)[1]

                            def k_mms(e, k0=k0, si=si):
                                last = None
                                for wt, hl in ((wph, 0), (wpl, 0), (wph, 1)):
                                    for c in range(8):
                                        rx_ap, _ = rx_slice(hl, c, k0, 256)
                                        first = (wt is wph and hl == 0 and c == 0)
                                        lastf = (wt is wph and hl == 1 and c == 7)
                                        last = e.matmul(
                                            psKQ[si][:, 0:256],
                                            wt[:, c * 128:(c + 1) * 128],
                                            rx_ap, start=first, stop=lastf)
                                return last
                            sch.op("tensor", k_mms, reads=[r_wp, rsrc],
                                   writes=[r_psKQ[si]])
                            nn = n
                            sch.op("scalar", lambda e, nn=nn, si=si, pair=pair:
                                   e.activation(
                                       kst[0][0:64, nn * 256:(nn + 1) * 256],
                                       psKQ[si][0:64, 0:256], AF.Identity,
                                       bias=bk_sb[0:64, pair:pair + 1]),
                                   reads=[r_psKQ[si], r_bk], writes=[r_kst[0]])
                            sch.op("vector", lambda e, nn=nn, si=si, pair=pair:
                                   e.scalar_tensor_tensor(
                                       kstg[0:64, nn * 256:(nn + 1) * 256],
                                       psKQ[si][0:64, 0:256],
                                       bk_sb[0:64, pair:pair + 1],
                                       kst[0][0:64, nn * 256:(nn + 1) * 256],
                                       ALU.add, ALU.subtract),
                                   reads=[r_psKQ[si], r_bk, r_kst[0]],
                                   writes=[r_kstg])
                            sch.op("scalar", lambda e, nn=nn, si=si, pair=pair:
                                   e.activation(
                                       kstg[64:128, nn * 256:(nn + 1) * 256],
                                       psKQ[si][64:128, 0:256], AF.Identity,
                                       bias=bk_sb[64:128, pair:pair + 1]),
                                   reads=[r_psKQ[si], r_bk], writes=[r_kstg])
                            sch.op("vector", lambda e, nn=nn, si=si, pair=pair:
                                   e.scalar_tensor_tensor(
                                       kst[1][64:128, nn * 256:(nn + 1) * 256],
                                       psKQ[si][64:128, 0:256],
                                       bk_sb[64:128, pair:pair + 1],
                                       kstg[64:128, nn * 256:(nn + 1) * 256],
                                       ALU.add, ALU.subtract),
                                   reads=[r_psKQ[si], r_bk, r_kstg],
                                   writes=[r_kst[1]])
                            if n % 2 == 1:
                                g8 = n // 2
                                gram_group(0, kst[0][0:64, :], r_kst[0],
                                           g8, NKT, 16)
                                gram_group(1, kstg[64:128, :], r_kstg,
                                           g8, NKT, 32)
                        sch.op("gpsimd", lambda e: e.dma_start(
                            out=kst[0][64:128, :], in_=kstg[0:64, :]),
                            reads=[r_kstg], writes=[r_kasm[0]], is_dma=True)
                        sch.op("gpsimd", lambda e: e.dma_start(
                            out=kst[1][0:64, :], in_=kstg[64:128, :]),
                            reads=[r_kstg], writes=[r_kasm[1]], is_dma=True)

                        for n in range(4):
                            q0 = n * 512
                            si = n % 2

                            def q_mms(e, q0=q0, si=si):
                                last = None
                                for wt, hl in ((wqh, 0), (wql, 0), (wqh, 1)):
                                    for c in range(8):
                                        rx_ap, _ = rx_slice(hl, c, q0, 512)
                                        first = (wt is wqh and hl == 0 and c == 0)
                                        lastf = (wt is wqh and hl == 1 and c == 7)
                                        last = e.matmul(
                                            psKQ[si][:, :],
                                            wt[:, c * 128:(c + 1) * 128],
                                            rx_ap, start=first, stop=lastf)
                                return last
                            sch.op("tensor", q_mms, reads=[r_wq, r_rxm],
                                   writes=[r_psKQ[si]])
                            nn = n
                            sch.op("scalar", lambda e, nn=nn, si=si, pair=pair:
                                   e.activation(
                                       qst[0][0:64, nn * 512:(nn + 1) * 512],
                                       psKQ[si][0:64, :], AF.Identity,
                                       bias=bq_sb[0:64, pair:pair + 1]),
                                   reads=[r_psKQ[si], r_bq], writes=[r_qst[0]])
                            sch.op("vector", lambda e, nn=nn, si=si, pair=pair:
                                   e.scalar_tensor_tensor(
                                       qlo[0][:, nn * 512:(nn + 1) * 512],
                                       psKQ[si][0:64, :],
                                       bq_sb[0:64, pair:pair + 1],
                                       qst[0][0:64, nn * 512:(nn + 1) * 512],
                                       ALU.add, ALU.subtract),
                                   reads=[r_psKQ[si], r_bq, r_qst[0]],
                                   writes=[r_qlo[0]])
                            sch.op("scalar", lambda e, nn=nn, si=si, pair=pair:
                                   e.activation(
                                       qst[1][64:128, nn * 512:(nn + 1) * 512],
                                       psKQ[si][64:128, :], AF.Identity,
                                       bias=bq_sb[64:128, pair:pair + 1]),
                                   reads=[r_psKQ[si], r_bq], writes=[r_qst[1]])
                            sch.op("vector", lambda e, nn=nn, si=si, pair=pair:
                                   e.scalar_tensor_tensor(
                                       qsg[64:128, nn * 512:(nn + 1) * 512],
                                       psKQ[si][64:128, :],
                                       bq_sb[64:128, pair:pair + 1],
                                       qst[1][64:128, nn * 512:(nn + 1) * 512],
                                       ALU.add, ALU.subtract),
                                   reads=[r_psKQ[si], r_bq, r_qst[1]],
                                   writes=[r_qsg])
                            if n == 1:
                                gram_group(0, qst[0][0:64, :], r_qst[0],
                                           0, NQB, 24)
                                gram_group(1, qst[1][64:128, :], r_qst[1],
                                           0, NQB, 40)
                                gram_final_both()
                        sch.op("gpsimd", lambda e: e.dma_start(
                            out=qst[0][64:128, :], in_=qst[0][0:64, :]),
                            reads=[r_qst[0]], writes=[r_qasm[0]], is_dma=True)
                        sch.op("gpsimd", lambda e: e.dma_start(
                            out=qst[1][0:64, :], in_=qst[1][64:128, :]),
                            reads=[r_qst[1]], writes=[r_qasm[1]], is_dma=True)
                        sch.op("gpsimd", lambda e: e.dma_start(
                            out=qlo[1][:, :], in_=qsg[64:128, 0:TQ]),
                            reads=[r_qsg], writes=[r_qlo[1]], is_dma=True)

                        if pair < 7:
                            load_pw(1, pair + 1)
                            load_qw(pair + 1)
                        if pi == 1 and g < H // VGH - 1:
                            load_vw(g + 1)
                        attn_pair(pair)

                if pending[0] is not None:
                    pending[0]()
                    pending[0] = None
                if "a" in tap_aps:
                    sch.op("sync", lambda e: e.dma_start(out=tap_aps["a"], in_=da),
                           reads=[r_da], writes=[r_out], is_dma=True)
                sch.barrier()

        if phases >= 3:
            with ExitStack() as phc:
                pc = phc.enter_context
                ahs = pc(nc.sbuf_tensor("ahs", [128, 8 * TQ], BF16))
                als = pc(nc.sbuf_tensor("als", [128, 8 * TQ], BF16))
                woh = pc(nc.sbuf_tensor("woh", [128, 8 * D], BF16))
                wol = pc(nc.sbuf_tensor("wol", [128, 8 * D], BF16))
                NC = 4
                x1c = [pc(nc.sbuf_tensor(f"x1c{i}", [128, D], F32)) for i in range(NC)]
                x2c = [pc(nc.sbuf_tensor(f"x2c{i}", [128, D], F32)) for i in range(NC)]
                xb3 = [pc(nc.sbuf_tensor(f"xb3_{i}", [128, D], BF16)) for i in range(NC)]
                xqT3 = [pc(nc.sbuf_tensor(f"xqT3_{i}", [128, D], FP8)) for i in range(NC)]
                yb3 = [pc(nc.sbuf_tensor(f"yb3_{i}", [128, D], BF16)) for i in range(NC)]
                oout = [pc(nc.sbuf_tensor(f"oout{i}", [128, D], F32)) for i in range(NC)]
                st3 = [pc(nc.sbuf_tensor(f"st3_{i}", [128, 12], F32)) for i in range(NC)]
                psO = [pc(nc.psum_tensor(f"psO{i}", [128, D], F32))
                       for i in range(2)]
                psT3 = [pc(nc.psum_tensor(f"psT3_{i}", [128, D], BF16))
                        for i in range(2)]
                psY3 = pc(nc.psum_tensor("psY3", [128, D], F32))

                r_ahs, r_als, r_woh, r_wol = R(), R(), R(), R()
                r_x1c = [R() for _ in range(NC)]
                r_x2c = [R() for _ in range(NC)]
                r_xb3 = [R() for _ in range(NC)]
                r_xqT3 = [R() for _ in range(NC)]
                r_yb3 = [R() for _ in range(NC)]
                r_oout = [R() for _ in range(NC)]
                r_st3 = [R() for _ in range(NC)]
                r_psO = [R(), R()]
                r_psT3 = [R(), R()]
                r_psY3 = R("psY3")

                for c in range(8):
                    sch.op("sync", lambda e, c=c: e.dma_start(
                        out=ahs[:, c * TQ:(c + 1) * TQ],
                        in_=da[0, c * 128:(c + 1) * 128, :]),
                        reads=[r_da], writes=[r_ahs], is_dma=True)
                    sch.op("sync", lambda e, c=c: e.dma_start(
                        out=als[:, c * TQ:(c + 1) * TQ],
                        in_=da[1, c * 128:(c + 1) * 128, :]),
                        reads=[r_da], writes=[r_als], is_dma=True)
                    sch.op("sync", lambda e, c=c: e.dma_start(
                        out=woh[:, c * D:(c + 1) * D], in_=dwosp[0, c, :, :]),
                        reads=[r_dwosp], writes=[r_woh], is_dma=True)
                    sch.op("sync", lambda e, c=c: e.dma_start(
                        out=wol[:, c * D:(c + 1) * D], in_=dwosp[1, c, :, :]),
                        reads=[r_dwosp], writes=[r_wol], is_dma=True)

                def layer_norm3(src_ap, r_src, st, r_st, dst_ap, r_dst):
                    sch.op("vector", lambda e: e.reduce_sum(
                        st[:, 0:1], src_ap, axis=AXL.X),
                        reads=[r_src], writes=[r_st])
                    sch.op("vector", lambda e: e.scalar_tensor_tensor(
                        ln_dump[:, :], src_ap, 1.0, src_ap, ALU.mult, ALU.mult,
                        accum_out=st[:, 1:2]),
                        reads=[r_src], writes=[r_dump, r_st])
                    sch.op("vector", lambda e: e.tensor_scalar(
                        st[:, 2:3], st[:, 0:1], -1.0 / D, None, ALU.mult),
                        reads=[r_st], writes=[r_st])
                    sch.op("vector", lambda e: e.scalar_tensor_tensor(
                        st[:, 3:4], st[:, 2:3], 1.0, st[:, 2:3], ALU.mult,
                        ALU.mult),
                        reads=[r_st], writes=[r_st])
                    sch.op("vector", lambda e: e.scalar_tensor_tensor(
                        st[:, 4:5], st[:, 3:4], -D, st[:, 1:2], ALU.mult,
                        ALU.add),
                        reads=[r_st], writes=[r_st])
                    sch.op("vector", lambda e: e.tensor_scalar(
                        st[:, 4:5], st[:, 4:5], 1.0 / D, None, ALU.mult),
                        reads=[r_st], writes=[r_st])
                    sch.op("vector", lambda e: e.tensor_scalar(
                        st[:, 9:10], st[:, 4:5], EPS, None, ALU.add),
                        reads=[r_st], writes=[r_st])
                    sch.op("scalar", lambda e: e.activation(
                        st[:, 5:6], st[:, 9:10], AF.Ln),
                        reads=[r_st], writes=[r_st])
                    sch.op("scalar", lambda e: e.activation(
                        st[:, 6:7], st[:, 5:6], AF.Exp, scale=-0.5),
                        reads=[r_st], writes=[r_st])
                    sch.op("vector", lambda e: e.scalar_tensor_tensor(
                        st[:, 7:8], st[:, 6:7], 1.0, st[:, 6:7], ALU.mult,
                        ALU.mult),
                        reads=[r_st], writes=[r_st])
                    sch.op("vector", lambda e: e.scalar_tensor_tensor(
                        st[:, 8:9], st[:, 9:10], 1.0, st[:, 7:8], ALU.mult,
                        ALU.mult),
                        reads=[r_st], writes=[r_st])
                    sch.op("vector", lambda e: e.tensor_scalar(
                        st[:, 8:9], st[:, 8:9], -0.5, 1.5, ALU.mult, ALU.add),
                        reads=[r_st], writes=[r_st])
                    sch.op("vector", lambda e: e.scalar_tensor_tensor(
                        st[:, 6:7], st[:, 8:9], 1.0, st[:, 6:7], ALU.mult,
                        ALU.mult),
                        reads=[r_st], writes=[r_st])
                    sch.op("vector", lambda e: e.tensor_scalar(
                        dst_ap, src_ap, st[:, 2:3], st[:, 6:7], ALU.add,
                        ALU.mult),
                        reads=[r_src, r_st], writes=[r_dst])

                def cstage0(t):
                    i = t % NC
                    sch.op("sync", lambda e, t=t, i=i: e.dma_start(
                        out=x1c[i][:, :], in_=dx1[t * 128:(t + 1) * 128, :]),
                        reads=[r_dx1], writes=[r_x1c[i]], is_dma=True)

                def cstage1(t):
                    i = t % NC
                    ob = t % 2
                    def o_mms(e, t=t, ob=ob):
                        last = None
                        for half in range(2):
                            o0 = half * 512
                            for ws, hl in ((woh, 0), (wol, 0), (woh, 1)):
                                for c in range(8):
                                    a_ap = (ahs if hl == 0 else als)[
                                        :, c * TQ + t * 128: c * TQ + (t + 1) * 128]
                                    first = (ws is woh and hl == 0 and c == 0)
                                    last = e.matmul(
                                        psO[ob][:, o0:o0 + 512], a_ap,
                                        ws[:, c * D + o0: c * D + o0 + 512],
                                        start=first, stop=False)
                            last = e.matmul(psO[ob][:, o0:o0 + 512],
                                            ones2[0:2, 0:128],
                                            borows[0:2, o0:o0 + 512],
                                            start=False, stop=True)
                        return last
                    sch.op("tensor", o_mms,
                           reads=[r_ahs, r_als, r_woh, r_wol, r_ones2, r_borows],
                           writes=[r_psO[ob]])
                    sch.op("vector", lambda e, i=i, ob=ob: e.tensor_add(
                        x2c[i][:, :], x1c[i][:, :], psO[ob][:, :]),
                        reads=[r_x1c[i], r_psO[ob]], writes=[r_x2c[i]])

                def cstage2(t):
                    i = t % NC
                    tb3 = t % 2
                    layer_norm3(x2c[i][:, :], r_x2c[i], st3[i], r_st3[i],
                                xb3[i][:, :], r_xb3[i])
                    for c in range(8):
                        sch.op("tensor", lambda e, i=i, c=c, tb3=tb3: e.transpose(
                            psT3[tb3][:, c * 128:(c + 1) * 128],
                            xb3[i][:, c * 128:(c + 1) * 128], idb_sb[:, :]),
                            reads=[r_xb3[i], r_idb], writes=[r_psT3[tb3]])
                    sch.op("vector", lambda e, i=i, tb3=tb3: e.tensor_scalar(
                        xqT3[i][:, :], psT3[tb3][:, 0:D], sinv_sb[:, 0:1], None,
                        ALU.mult),
                        reads=[r_psT3[tb3], r_sinv], writes=[r_xqT3[i]])
                    sch.op("vector", lambda e, i=i: e.tensor_copy(
                        xqT3[i][:, :].bitcast(I32), xqT3[i][:, :].bitcast(I32)),
                        reads=[r_xqT3[i]], writes=[r_xqT3[i]])

                def cstage3(t):
                    i = t % NC
                    for half in range(2):
                        for c in range(8):
                            sch.op("tensor", lambda e, i=i, c=c, half=half: e.matmul(
                                psY3[:, half * 512:(half + 1) * 512],
                                xqT3[i][:, c * 128:(c + 1) * 128],
                                kern8[:, c * D + half * 512:c * D + half * 512 + 512],
                                start=(c == 0), stop=(c == 7)),
                                reads=[r_xqT3[i], r_kern8], writes=[r_psY3])
                    for half in range(2):
                        sch.op("vector", lambda e, i=i, half=half:
                               e.scalar_tensor_tensor(
                                   yb3[i][:, half * 512:(half + 1) * 512],
                                   psY3[:, half * 512:(half + 1) * 512],
                                   stot_sb[:, 0:1],
                                   fb_sb[:, half * 512:(half + 1) * 512],
                                   ALU.mult, ALU.add),
                               reads=[r_psY3, r_stot, r_fb],
                               writes=[r_yb3[i]])
                    sch.op("vector", lambda e, i=i: e.tensor_copy(
                        yb3[i][:, :].bitcast(I32), yb3[i][:, :].bitcast(I32)),
                        reads=[r_yb3[i]], writes=[r_yb3[i]])
                    sch.op("vector", lambda e, i=i: e.scalar_tensor_tensor(
                        oout[i][:, :], yb3[i][:, :], 0.5, x2c[i][:, :],
                        ALU.mult, ALU.add),
                        reads=[r_yb3[i], r_x2c[i]], writes=[r_oout[i]])
                    sch.op("sync", lambda e, t=t, i=i: e.dma_start(
                        out=dout[t * 128:(t + 1) * 128, :], in_=oout[i][:, :]),
                        reads=[r_oout[i]], writes=[r_out], is_dma=True)

                for step in range(NTQ + 3):
                    for s, fn in ((3, cstage3), (2, cstage2), (1, cstage1),
                                  (0, cstage0)):
                        tt = step - s
                        if 0 <= tt < NTQ:
                            fn(tt)

        sch.op("sync", lambda e: e.nop(), reads=[r_out])

        with nc.Block() as block:
            sch.emit(nc, block)

    return nc, tap_aps


_ROPE_CACHE = {}


def _rope_tables():
    if "t" not in _ROPE_CACHE:
        half = D // 2
        freq = np.exp(np.arange(0, D, 2, dtype=np.float64) *
                      (-math.log(10000.0) / D))
        ang = np.arange(S, dtype=np.float64)[:, None] * freq[None, :]
        _ROPE_CACHE["t"] = (np.cos(ang).astype(np.float32),
                            np.sin(ang).astype(np.float32))
    return _ROPE_CACHE["t"]


_PROG_CACHE = {}


def _get_prog():
    if "p" not in _PROG_CACHE:
        _PROG_CACHE["p"] = build_program()[0]
    return _PROG_CACHE["p"]


def kernel(L0_attn_x, L0_attn_Wq, L0_attn_bq, L0_attn_Wk, L0_attn_bk,
           L0_attn_Wv, L0_attn_bv, L0_attn_Wo, L0_attn_bo, L0_ffn_kernel,
           L0_ffn_input_scale, L0_ffn_kernel_scale, L0_ffn_bias):
    x = np.ascontiguousarray(np.asarray(L0_attn_x, dtype=np.float32))
    wqt = np.ascontiguousarray(np.asarray(L0_attn_Wq, dtype=np.float32).T)
    wkt = np.ascontiguousarray(np.asarray(L0_attn_Wk, dtype=np.float32).T)
    wvt = np.ascontiguousarray(np.asarray(L0_attn_Wv, dtype=np.float32).T)
    wot = np.ascontiguousarray(np.asarray(L0_attn_Wo, dtype=np.float32).T)
    kern = np.asarray(L0_ffn_kernel)
    if kern.dtype != ml_dtypes.bfloat16:
        kern = kern.view(ml_dtypes.bfloat16) if kern.dtype == np.uint16 else \
            kern.astype(ml_dtypes.bfloat16)
    kern = np.ascontiguousarray(kern)
    bq = np.asarray(L0_attn_bq, dtype=np.float32)
    bk = np.asarray(L0_attn_bk, dtype=np.float32)
    bv = np.asarray(L0_attn_bv, dtype=np.float32)
    bo = np.asarray(L0_attn_bo, dtype=np.float32)
    fbias = np.asarray(L0_ffn_bias)
    if fbias.dtype != ml_dtypes.bfloat16:
        fbias = fbias.view(ml_dtypes.bfloat16) if fbias.dtype == np.uint16 else \
            fbias.astype(ml_dtypes.bfloat16)
    s_in = np.asarray(L0_ffn_input_scale, dtype=np.float32).reshape(1)
    s_k = np.asarray(L0_ffn_kernel_scale, dtype=np.float32).reshape(1)

    cos_t, sin_t = _rope_tables()
    bqp = np.ascontiguousarray(bq.reshape(8, 128).T)
    bkp = np.ascontiguousarray(bk.reshape(8, 128).T)
    bvr = np.ascontiguousarray(bv.reshape(1, D))
    bor = np.ascontiguousarray(bo.reshape(1, D))
    fb_t = np.ascontiguousarray(np.tile(fbias.reshape(1, D), (128, 1)))
    id8 = np.eye(128, dtype=ml_dtypes.float8_e4m3)
    idb = np.eye(128, dtype=ml_dtypes.bfloat16)
    s_in_bc = np.ascontiguousarray(np.tile(s_in.reshape(1, 1), (128, 1)))
    s_k_bc = np.ascontiguousarray(np.tile(s_k.reshape(1, 1), (128, 1)))

    in_maps = []
    for c in range(8):
        b = c // 2
        j = c % 2
        perm = np.r_[j * TQ:(j + 1) * TQ, (1 - j) * TQ:(2 - j) * TQ]
        in_maps.append({
            "x": np.ascontiguousarray(x[b][perm]),
            "wqt": wqt, "wkt": wkt, "wvt": wvt, "wot": wot,
            "kern": kern, "bqp": bqp, "bkp": bkp, "bvr": bvr, "bor": bor,
            "fbias": fb_t,
            "cos_t": np.ascontiguousarray(cos_t[perm]),
            "sin_t": np.ascontiguousarray(sin_t[perm]),
            "id8": id8, "idb": idb, "s_in": s_in_bc, "s_k": s_k_bc,
        })

    nc = _get_prog()
    res = run_bass_kernel_spmd(nc, in_maps, core_ids=list(range(8)))
    out = np.empty((B, S, D), dtype=np.float32)
    for c in range(8):
        b, j = c // 2, c % 2
        out[b, j * TQ:(j + 1) * TQ, :] = np.asarray(res.results[c]["out"])
    return out
